# version 05
# baseline (speedup 1.0000x reference)
import numpy as np
from contextlib import ExitStack

import concourse.bass as bass
import concourse.mybir as mybir
from concourse.bass_utils import run_bass_kernel_spmd

F32 = mybir.dt.float32
BF16 = mybir.dt.bfloat16
AF = mybir.ActivationFunctionType
ALU = mybir.AluOpType
AX = mybir.AxisListType

P = 128
DM, DI, DFF, R, KW, G = 1024, 2048, 4096, 256, 4, 8
NMC, NIC, NFC, NRC = DM // P, DI // P, DFF // P, R // P
EPS = 1e-5
NCORES = 8
TOK = 2048
OWN = [410, 410, 410, 409, 409]
STARTS = [0, 410, 820, 1230, 1639]
NT = len(OWN)
CW = 416
MAGIC = 0x5F3759DF
I32 = mybir.dt.int32

ENGS = ("tensor", "vector", "scalar", "sync", "gpsimd")
DEBUG = False
STOP_AFTER = None
ATTN_GROUPS = None
ATTN_PHASE = None


class Em:

    def __init__(self):
        self.q = {e: [] for e in ENGS}
        self.cnt = {}
        self.waited = {}

    def now(self, s):
        return self.cnt.setdefault(s, 0)

    def emit(self, eng, fn=None, waits=(), incs=()):
        wl = []
        for s, v in waits:
            self.now(s)
            if v is None:
                v = self.cnt[s]
            if v > 0 and v > self.waited.get((eng, s), 0):
                assert v <= self.cnt[s], f"wait {s}>={v} exceeds emitted {self.cnt[s]}"
                self.waited[(eng, s)] = v
                wl.append((s, v))
        out = {}
        il = []
        for s, k in incs:
            self.cnt[s] = self.now(s) + k
            out[s] = self.cnt[s]
            il.append((s, k))
        if fn is None and not il and not wl:
            return out
        self.q[eng].append((fn, wl, il))
        return out


def bank32(ps, b, n=1):
    return ps.ap()[:, 512 * b: 512 * (b + n)]


def bank16(ps, b, n=1):
    return ps.ap().bitcast(BF16)[:, 1024 * b: 1024 * (b + n)]


def tile_dims(t):
    V = OWN[t]
    C = V + 3
    bw = [min(P, C - P * j) for j in range(4)]
    return C, V, bw


def build_nc():
    nc = bass.Bass("TRN2", target_bir_lowering=False)

    d_x = nc.dram_tensor("x_in", [TOK + 3, DM], F32, kind="ExternalInput").ap()
    d_mask = nc.dram_tensor("mask_in", [P, 4], F32, kind="ExternalInput").ap()
    d_u1 = nc.dram_tensor("u1_in", [P, NMC * R], F32, kind="ExternalInput").ap()
    d_v1 = nc.dram_tensor("v1_in", [P, NRC * DFF], F32, kind="ExternalInput").ap()
    d_u2 = nc.dram_tensor("u2_in", [P, NFC * R], F32, kind="ExternalInput").ap()
    d_v2 = nc.dram_tensor("v2_in", [P, NRC * DM], F32, kind="ExternalInput").ap()
    d_ip = nc.dram_tensor("ip_in", [P, G * 4096], F32, kind="ExternalInput").ap()
    d_cv = nc.dram_tensor("cv_in", [P, G * 2048], F32, kind="ExternalInput").ap()
    d_fc = nc.dram_tensor("fc_in", [P, NMC * 2048], F32, kind="ExternalInput").ap()
    d_cp = nc.dram_tensor("cp_in", [P, NIC * 1024], F32, kind="ExternalInput").ap()
    d_ow = nc.dram_tensor("ow_in", [P, NMC * 2048], F32, kind="ExternalInput").ap()
    d_b1 = nc.dram_tensor("b1_in", [P, NFC], F32, kind="ExternalInput").ap()
    d_b2 = nc.dram_tensor("b2_in", [P, NMC], F32, kind="ExternalInput").ap()
    d_cb = nc.dram_tensor("cb_in", [P, NIC], F32, kind="ExternalInput").ap()
    d_fb = nc.dram_tensor("fb_in", [P, NMC], F32, kind="ExternalInput").ap()
    d_cpb = nc.dram_tensor("cpb_in", [P, NIC], F32, kind="ExternalInput").ap()
    d_lg = nc.dram_tensor("lg_in", [P, NIC], F32, kind="ExternalInput").ap()
    d_lb = nc.dram_tensor("lb_in", [P, NIC], F32, kind="ExternalInput").ap()

    d_out = nc.dram_tensor("x_out", [TOK, DM], F32, kind="ExternalOutput").ap()

    s_ip = nc.dram_tensor("s_ip", [P, G * 4096], BF16, kind="Internal").ap()
    s_cv = nc.dram_tensor("s_cv", [P, G * 2048], BF16, kind="Internal").ap()
    s_fc = nc.dram_tensor("s_fc", [P, NMC * 2048], BF16, kind="Internal").ap()
    s_cp = nc.dram_tensor("s_cp", [P, NIC * 1024], BF16, kind="Internal").ap()
    s_ow = nc.dram_tensor("s_ow", [P, NMC * 2048], BF16, kind="Internal").ap()

    A = nc.alloc_sbuf_tensor
    xA = A("xA", [P, 4 * DM], F32).ap()
    xB = A("xB", [P, 4 * DM], F32).ap()
    xntm = A("xntm", [P, 4 * DM], BF16).ap()
    sfc = xntm[:, 0:NMC * CW]
    xnfm = A("xnfm", [P, NMC * CW], BF16).ap()
    scr32 = A("scr32", [P, NMC * CW], F32).ap()
    pfm = A("pfm", [P, NRC * CW], BF16).ap()
    qfm = A("qfm", [P, NRC * CW], BF16).ap()
    hh = A("hh", [P, 4 * CW], BF16).ap()
    hgt = A("hgt", [P, NIC * CW], BF16).ap()
    hnb = A("hnb", [P, NIC * CW], BF16).ap()
    hgrp = A("hgrp", [P, 2 * 2 * CW], BF16).ap()
    sgrp = A("sgrp", [P, 2 * CW], BF16).ap()
    sqpp = A("sqpp", [P, 2 * CW], BF16).ap()

    u1w = A("u1w", [P, NMC * R], BF16).ap()
    v1w = A("v1w", [P, NRC * DFF], BF16).ap()
    u2w = A("u2w", [P, NFC * R], BF16).ap()
    v2w = A("v2w", [P, NRC * DM], BF16).ap()
    wip = A("wip", [P, 2 * 4096], BF16).ap()
    wcv = A("wcv", [P, 2 * 2048], BF16).ap()
    wfc = A("wfc", [P, 2 * 2048], BF16).ap()
    wcp = A("wcp", [P, 2 * 1024], BF16).ap()
    wow = A("wow", [P, 2 * 2048], BF16).ap()

    b1c = A("b1c", [P, NFC], F32).ap()
    b2h = A("b2h", [P, NMC], F32).ap()
    b2c = A("b2c", [P, NMC], F32).ap()
    cbc = A("cbc", [P, NIC], F32).ap()
    fbc = A("fbc", [P, NMC], F32).ap()
    cpc = A("cpc", [P, NIC], F32).ap()
    lgc = A("lgc", [P, NIC], F32).ap()
    lbc = A("lbc", [P, NIC], F32).ap()
    maskb = A("maskb", [P, 4], BF16).ap()

    idb = A("idb", [P, P], BF16).ap()
    idf = A("idf", [P, P], F32).ap()
    onesb = A("onesb", [P, 1], BF16).ap()
    onesr = A("onesr", [1, P], F32).ap()
    epsb = A("epsb", [P, 1], F32).ap()
    halfc = A("halfc", [P, 1], F32).ap()

    xC = A("xC", [P, 4 * DM], F32).ap()
    bn6 = A("bn6", [P, 48], F32).ap()
    st2_sm = A("st2_sm", [P, 4], F32).ap()
    st2_sq = A("st2_sq", [P, 4], F32).ap()
    st2_mu = A("st2_mu", [P, 4], F32).ap()
    st2_var = A("st2_var", [P, 4], F32).ap()
    st2_sd = A("st2_sd", [P, 4], F32).ap()
    st2_rs = A("st2_rs", [P, 4], F32).ap()
    st2_am = A("st2_am", [P, 4], F32).ap()
    st_s1 = A("st_s1", [P, 4], F32).ap()
    st_s2 = A("st_s2", [P, 4], F32).ap()
    st_mu = A("st_mu", [P, 4], F32).ap()
    st_m2 = A("st_m2", [P, 4], F32).ap()
    st_var = A("st_var", [P, 4], F32).ap()
    st_sd = A("st_sd", [P, 4], F32).ap()
    st_rs = A("st_rs", [P, 4], F32).ap()
    st_am = A("st_am", [P, 4], F32).ap()
    lnT = A("lnT", [P, 8], F32).ap()
    row_sums = A("row_sums", [1, 2 * CW], F32).ap()
    row_ra = A("row_ra", [1, 2 * CW], F32).ap()

    ps = nc.alloc_psum_tensor("ps", [P, 4096], F32)

    T = dict(locals())
    T["ipmm"], T["cvmm"], T["fcmm"], T["cpmm"], T["owmm"] = {}, {}, {}, {}, {}

    E = Em()
    emit_prologue(E, T)
    for t in range(NT):
        emit_tile(E, T, t)
    if DEBUG:
        for name in ("xA", "xB", "hgt", "hnb", "xnfm", "scr32", "st_s1", "st_s2",
                     "st_rs", "st_am", "lnT", "row_sums", "row_ra", "xntm", "sgrp",
                     "hgrp", "pfm", "qfm", "st_mu", "st_m2", "st_var", "st_sd",
                     "epsb", "halfc", "xC", "bn6", "st2_rs", "st2_am", "st2_mu",
                     "st2_var"):
            ap = T[name]
            sh = list(ap.shape)
            dbg = nc.dram_tensor("dbg_" + name, sh, F32, kind="ExternalOutput").ap()
            E.emit("gpsimd", lambda e, ap=ap, dbg=dbg: e.dma_start(out=dbg, in_=ap),
                   waits=[("eV", E.now("eV")), ("eS", E.now("eS")),
                          ("eT", E.now("eT"))],
                   incs=[("dOU", 16)])
    E.emit("gpsimd", None, waits=[("dOU", E.now("dOU"))])

    with ExitStack() as st:
        sems = {s: st.enter_context(nc.semaphore(s)) for s in sorted(E.cnt)}
        block = st.enter_context(nc.Block())
        for ename in ENGS:
            items = E.q[ename]
            if not items:
                continue

            def runner(eng, items=items):
                for fn, wl, il in items:
                    for s, v in wl:
                        eng.wait_ge(sems[s], v)
                    if fn is not None:
                        inst = fn(eng)
                        for s, k in il:
                            inst.then_inc(sems[s], k)

            getattr(block, ename)(runner)
    return nc



def emit_prologue(E, T):
    g = "gpsimd"
    for name in ("xA", "xB", "xntm", "xnfm", "scr32", "pfm", "qfm", "hh",
                 "hgt", "hnb", "hgrp", "sgrp", "sqpp", "row_sums", "row_ra",
                 "lnT", "st_s1", "st_s2", "st_mu", "st_m2", "st_var", "st_sd",
                 "st_rs", "st_am", "xC", "bn6", "st2_sm", "st2_sq", "st2_mu",
                 "st2_var", "st2_sd", "st2_rs", "st2_am", "idb", "idf"):
        E.emit(g, lambda e, n=name: e.memset(T[n], 0.0), incs=[("eG", 1)])
    E.emit(g, lambda e: e.affine_select(
        out=T["idb"], in_=T["idb"], compare_op=ALU.not_equal, fill=1.0,
        base=0, pattern=[[-1, P]], channel_multiplier=1), incs=[("eG", 1)])
    E.emit(g, lambda e: e.affine_select(
        out=T["idf"], in_=T["idf"], compare_op=ALU.not_equal, fill=1.0,
        base=0, pattern=[[-1, P]], channel_multiplier=1), incs=[("eG", 1)])
    E.emit(g, lambda e: e.memset(T["onesb"], 1.0), incs=[("eG", 1)])
    E.emit(g, lambda e: e.memset(T["onesr"], 1.0), incs=[("eG", 1)])
    E.emit(g, lambda e: e.memset(T["epsb"], EPS), incs=[("eG", 1)])
    E.emit(g, lambda e: e.memset(T["halfc"], 0.5), incs=[("eG", 1)])
    T["ev_setup"] = E.now("eG")
    E.emit(g, lambda e: e.dma_start(out=T["maskb"], in_=T["d_mask"]), incs=[("dMK", 16)])
    T["ev_mask"] = E.now("dMK")
    for dst, src in (("u1w", "d_u1"), ("v1w", "d_v1"), ("u2w", "d_u2"), ("v2w", "d_v2")):
        E.emit(g, lambda e, d=dst, s=src: e.dma_start(out=T[d], in_=T[s]),
               incs=[("dFW", 16)])
    T["ev_ffnw"] = E.now("dFW")
    for dst, src, sem in (("s_ip", "d_ip", "dSIP"), ("s_cv", "d_cv", "dSCV"),
                          ("s_fc", "d_fc", "dSFC"), ("s_cp", "d_cp", "dSCP"),
                          ("s_ow", "d_ow", "dSOW")):
        ev = E.emit(g, lambda e, d=dst, s=src: e.dma_start(out=T[d], in_=T[s]),
                    incs=[(sem, 16)])
        T["ev_" + dst] = (sem, ev[sem])
    for dst, src in (("b1c", "d_b1"), ("b2c", "d_b2"), ("cbc", "d_cb"),
                     ("fbc", "d_fb"), ("cpc", "d_cpb"), ("lgc", "d_lg"),
                     ("lbc", "d_lb")):
        E.emit("sync", lambda e, d=dst, s=src: e.dma_start(out=T[d], in_=T[s]),
               incs=[("dBI", 16)])
    T["ev_bias"] = E.now("dBI")
    E.emit("vector", lambda e: e.tensor_scalar_mul(T["b2h"], T["b2c"], 0.5),
           waits=[("dBI", T["ev_bias"])], incs=[("eV", 1)])
    T["ev_b2h"] = E.now("eV")
    emit_xin(E, T, 0)
    emit_norm1_stats(E, T, 0)
    emit_norm1_apply(E, T, 0, waits_eT=0)



def emit_rsqrt(E, T, y, v, tmp, iters=3):
    E.emit("vector", lambda e: e.tensor_scalar(
        y.bitcast(I32), v.bitcast(I32), 1, -1,
        ALU.arith_shift_right, ALU.bitwise_xor),
        waits=[("eV", E.now("eV"))], incs=[("eV", 1)])
    E.emit("vector", lambda e: e.tensor_scalar(
        y.bitcast(I32), y.bitcast(I32), MAGIC + 1, None, ALU.add),
        waits=[("eV", E.now("eV"))], incs=[("eV", 1)])
    for _ in range(iters):
        E.emit("vector", lambda e: e.tensor_mul(tmp, y, y),
               waits=[("eV", E.now("eV"))], incs=[("eV", 1)])
        E.emit("vector", lambda e: e.tensor_mul(tmp, tmp, v),
               waits=[("eV", E.now("eV"))], incs=[("eV", 1)])
        E.emit("vector", lambda e: e.tensor_scalar(tmp, tmp, -0.5, 1.5,
                                                   ALU.mult, ALU.add),
               waits=[("eV", E.now("eV"))], incs=[("eV", 1)])
        E.emit("vector", lambda e: e.tensor_mul(y, y, tmp),
               waits=[("eV", E.now("eV"))], incs=[("eV", 1)])


def vflush(E):
    ev = E.emit("vector", lambda e: e.engine_nop(),
                waits=[("eV", E.now("eV"))], incs=[("eV", 1)])
    return ev["eV"]


def emit_xin(E, T, t):
    C, V, bw = tile_dims(t)
    r0 = STARTS[t]
    waits = []
    if t > 0:
        waits = [("eV", T["ev_xC_free_v"])]
    ev = None
    for j in range(4):
        ev = E.emit("gpsimd", lambda e, j=j, r0=r0, b=bw[j]: e.dma_start(
            out=T["xC"][0:b, DM * j: DM * (j + 1)],
            in_=T["d_x"][r0 + P * j: r0 + P * j + b, :]),
            waits=waits if j == 0 else (), incs=[("dXI", 16)])
    T["ev_xin"] = ev["dXI"]


def emit_norm1_stats(E, T, t):
    bn6 = T["bn6"]
    xC = T["xC"]
    for j in range(4):
        for h in range(2):
            E.emit("vector", lambda e, j=j, h=h: e.bn_stats(
                bn6[:, 12 * j + 6 * h: 12 * j + 6 * h + 6],
                xC[:, DM * j + 512 * h: DM * j + 512 * h + 512]),
                waits=[("dXI", T["ev_xin"])] if (j == 0 and h == 0) else (),
                incs=[("eV", 1)])
    b3 = bn6.rearrange("p (j c) -> p j c", c=12)
    sm, sq = T["st2_sm"], T["st2_sq"]
    mu, var, sd, rs, am = (T["st2_mu"], T["st2_var"], T["st2_sd"],
                           T["st2_rs"], T["st2_am"])
    E.emit("vector", lambda e: e.tensor_add(sm, b3[:, :, 1], b3[:, :, 4]),
           waits=[("eV", E.now("eV"))], incs=[("eV", 1)])
    E.emit("vector", lambda e: e.tensor_add(sm, sm, b3[:, :, 7]),
           waits=[("eV", E.now("eV"))], incs=[("eV", 1)])
    E.emit("vector", lambda e: e.tensor_add(sm, sm, b3[:, :, 10]),
           waits=[("eV", E.now("eV"))], incs=[("eV", 1)])
    E.emit("vector", lambda e: e.tensor_scalar_mul(mu, sm, 0.25),
           waits=[("eV", E.now("eV"))], incs=[("eV", 1)])
    E.emit("vector", lambda e: e.tensor_mul(sq, b3[:, :, 1], b3[:, :, 1]),
           incs=[("eV", 1)])
    E.emit("vector", lambda e: e.tensor_mul(var, b3[:, :, 4], b3[:, :, 4]),
           incs=[("eV", 1)])
    E.emit("vector", lambda e: e.tensor_add(sq, sq, var),
           waits=[("eV", E.now("eV"))], incs=[("eV", 1)])
    E.emit("vector", lambda e: e.tensor_mul(var, b3[:, :, 7], b3[:, :, 7]),
           incs=[("eV", 1)])
    E.emit("vector", lambda e: e.tensor_add(sq, sq, var),
           waits=[("eV", E.now("eV"))], incs=[("eV", 1)])
    E.emit("vector", lambda e: e.tensor_mul(var, b3[:, :, 10], b3[:, :, 10]),
           incs=[("eV", 1)])
    E.emit("vector", lambda e: e.tensor_add(sq, sq, var),
           waits=[("eV", E.now("eV"))], incs=[("eV", 1)])
    E.emit("vector", lambda e: e.tensor_add(var, b3[:, :, 2], b3[:, :, 5]),
           waits=[("eV", E.now("eV"))], incs=[("eV", 1)])
    E.emit("vector", lambda e: e.tensor_add(var, var, b3[:, :, 8]),
           waits=[("eV", E.now("eV"))], incs=[("eV", 1)])
    E.emit("vector", lambda e: e.tensor_add(var, var, b3[:, :, 11]),
           waits=[("eV", E.now("eV"))], incs=[("eV", 1)])
    E.emit("vector", lambda e: e.tensor_scalar(sm, sq, 0.25, None, ALU.mult),
           waits=[("eV", E.now("eV"))], incs=[("eV", 1)])
    E.emit("vector", lambda e: e.scalar_tensor_tensor(
        var, var, 1.0 / DM, sm, ALU.mult, ALU.add),
        waits=[("eV", E.now("eV"))], incs=[("eV", 1)])
    E.emit("vector", lambda e: e.tensor_mul(sq, mu, mu),
           incs=[("eV", 1)])
    E.emit("vector", lambda e: e.scalar_tensor_tensor(
        var, sq, -1.0, var, ALU.mult, ALU.add),
        waits=[("eV", E.now("eV"))], incs=[("eV", 1)])
    E.emit("vector", lambda e: e.tensor_scalar(var, var, EPS, None, ALU.add),
           waits=[("eV", E.now("eV"))], incs=[("eV", 1)])
    emit_rsqrt(E, T, rs, var, sd)
    E.emit("vector", lambda e: e.tensor_mul(am, mu, rs),
           waits=[("eV", E.now("eV"))], incs=[("eV", 1)])
    T["ev_n1chain"] = E.now("eV")


def emit_norm1_apply(E, T, t, waits_eT=None):
    if waits_eT is None:
        waits_eT = E.now("eT")
    evb = []
    for j in range(4):
        w = [("eV", T["ev_n1chain"])] if j == 0 else []
        if j == 0 and waits_eT:
            w.append(("eT", waits_eT))
        ev = E.emit("vector", lambda e, j=j: e.tensor_scalar(
            T["xntm"][:, DM * j: DM * (j + 1)], T["xC"][:, DM * j: DM * (j + 1)],
            T["st2_rs"][:, j:j + 1], T["st2_am"][:, j:j + 1],
            ALU.mult, ALU.subtract), waits=w, incs=[("eV", 1)])
        evb.append(ev["eV"])
    T["ev_apply_blk"] = evb
    T["ev_apply"] = E.now("eV")


def emit_norm23(E, T, t, src, which):
    x = T[src]
    E.emit("scalar", None, waits=[("eT", E.now("eT")), ("dBI", T["ev_bias"])])
    ev_sq = {}
    for j in range(4):
        jw = [("eV", T.get("ev_stt_blk%d" % j, 0))]
        ev = E.emit("scalar", lambda e, j=j: e.activation(
            T["scr32"][:, 0:DM], x[:, DM * j: DM * (j + 1)], AF.Square,
            accum_out=T["st_s2"][:, j:j + 1]), incs=[("eS", 1)], waits=jw)
        ev_sq[j] = ev["eS"]
    evb = [0, 0, 0, 0]
    for h in range(2):
        c = slice(2 * h, 2 * h + 2)
        E.emit("vector", lambda e, c=c: e.tensor_scalar_mul(
            T["st_mu"][:, c], T["st_s1"][:, c], 1.0 / DM),
            waits=[("eV", E.now("eV"))], incs=[("eV", 1)])
        E.emit("vector", lambda e, c=c: e.tensor_mul(
            T["st_m2"][:, c], T["st_mu"][:, c], T["st_mu"][:, c]),
            waits=[("eV", E.now("eV"))], incs=[("eV", 1)])
        E.emit("vector", lambda e, c=c: e.scalar_tensor_tensor(
            T["st_var"][:, c], T["st_s2"][:, c], 1.0 / DM, T["st_m2"][:, c],
            ALU.mult, ALU.subtract),
            waits=[("eS", ev_sq[2 * h + 1]), ("eV", E.now("eV"))], incs=[("eV", 1)])
        E.emit("vector", lambda e, c=c: e.tensor_scalar(
            T["st_var"][:, c], T["st_var"][:, c], EPS, None, ALU.add),
            waits=[("eV", E.now("eV"))], incs=[("eV", 1)])
        emit_rsqrt(E, T, T["st_rs"][:, c], T["st_var"][:, c], T["st_sd"][:, c])
        E.emit("vector", lambda e, c=c: e.tensor_mul(
            T["st_am"][:, c], T["st_mu"][:, c], T["st_rs"][:, c]),
            waits=[("eV", E.now("eV"))], incs=[("eV", 1)])
        if h == 0:
            E.emit("vector", None, waits=[("eT", E.now("eT"))])
        for j in (2 * h, 2 * h + 1):
            ev = E.emit("vector", lambda e, j=j: e.tensor_scalar(
                T["xntm"][:, DM * j: DM * (j + 1)], x[:, DM * j: DM * (j + 1)],
                T["st_rs"][:, j:j + 1], T["st_am"][:, j:j + 1],
                ALU.mult, ALU.subtract), incs=[("eV", 1)])
            evb[j] = ev["eV"]
    T["ev_apply_blk"] = evb
    T["ev_apply"] = E.now("eV")
    if which == 3:
        T["ev_xA_free_s"] = E.now("eS")
        T["ev_xA_free_v"] = T["ev_apply"]


def emit_xpose_xn(E, T, t):
    C, V, bw = tile_dims(t)
    evb = T["ev_apply_blk"]
    for j in range(4):
        b = 4 + (j % 2)
        w = bw[j]
        E.emit("tensor", None, waits=[("eV", T.get("ev_ptb%d" % (j % 2), 0)),
                                      ("eV", evb[j]),
                                      ("eG", T["ev_setup"])])
        for f in range(NMC):
            E.emit("tensor", lambda e, j=j, f=f, b=b: e.transpose(
                bank16(T["ps"], b)[:, P * f: P * (f + 1)],
                T["xntm"][:, DM * j + P * f: DM * j + P * (f + 1)],
                T["idb"]), incs=[("eT", 1)])
        ev_t = E.now("eT")
        ev = E.emit("vector", lambda e, j=j, b=b, w=w: e.tensor_copy(
            T["xnfm"].rearrange("p (f c) -> p f c", c=CW)[:, :, P * j: P * j + w],
            bank16(T["ps"], b).rearrange("p (f c) -> p f c", c=P)[:, :, 0:w]),
            waits=[("eT", ev_t)], incs=[("eV", 1)])
        T["ev_ptb%d" % (j % 2)] = ev["eV"]
    T["ev_xnfm"] = E.now("eV")


def emit_stream_add(E, T, t, dst, src_stream, accum):
    C, V, bw = tile_dims(t)
    ev_scr = E.now("eS")
    for j in range(4):
        bk = 0 if (j % 2 == 0) else 2
        w = bw[j]
        E.emit("tensor", None, waits=[("eV", T.get("ev_ptf%d" % (j % 2), 0)),
                                      ("eS", ev_scr)])
        for f in range(NMC):
            E.emit("tensor", lambda e, j=j, f=f, bk=bk, w=w: e.transpose(
                bank32(T["ps"], bk, 2)[0:w, P * f: P * (f + 1)],
                T["scr32"][:, CW * f + P * j: CW * f + P * j + w],
                T["idf"]), incs=[("eT", 1)])
        ev_t = E.now("eT")
        waits = [("eT", ev_t)]
        if dst is T["xB"] and T.get("ev_out_dma", 0):
            waits.append(("dOU", T["ev_out_dma"]))
        kw = {}
        if accum:
            kw = dict(accum_out=T["st_s1"][0:w, j:j + 1])
        ev = E.emit("vector", lambda e, j=j, bk=bk, w=w, kw=kw: e.scalar_tensor_tensor(
            dst[0:w, DM * j: DM * (j + 1)],
            bank32(T["ps"], bk, 2)[0:w, :],
            1.0,
            src_stream[0:w, DM * j: DM * (j + 1)],
            ALU.mult, ALU.add, **kw),
            waits=waits, incs=[("eV", 1)])
        T["ev_ptf%d" % (j % 2)] = ev["eV"]
        T["ev_stt_blk%d" % j] = ev["eV"]
    T["ev_stream"] = E.now("eV")


def emit_ffn(E, T, t, post_u1=None):
    C, V, bw = tile_dims(t)
    N = C
    ps = T["ps"]
    E.emit("tensor", None, waits=[("dFW", T["ev_ffnw"]), ("eV", T["ev_xnfm"]),
                                  ("eS", E.now("eS"))])
    for m in range(NRC):
        for k in range(NMC):
            E.emit("tensor", lambda e, m=m, k=k: e.matmul(
                bank32(ps, m)[:, 0:N],
                T["u1w"][:, R * k + P * m: R * k + P * (m + 1)],
                T["xnfm"][:, CW * k: CW * k + N],
                start=(k == 0), stop=(k == NMC - 1)), incs=[("eT", 1)])
    ev_u1 = E.now("eT")
    E.emit("scalar", lambda e: e.preload_act_table(AF.Gelu_apprx_tanh))
    for m in range(NRC):
        E.emit("scalar", lambda e, m=m: e.copy(
            T["pfm"][:, CW * m: CW * m + N], bank32(ps, m)[:, 0:N]),
            waits=[("eT", ev_u1)], incs=[("eS", 1)])
    ev_pfm = E.now("eS")
    if post_u1 is not None:
        post_u1()
    gelu_done, u2_done, v1_done = {}, {}, {}
    for f in range(NFC):
        pp = f % 4
        vb = pp
        tw = [("eS", ev_pfm)]
        if f >= 4:
            tw.append(("eS", gelu_done[f - 4]))
        E.emit("tensor", None, waits=tw)
        for k in range(NRC):
            E.emit("tensor", lambda e, f=f, k=k, vb=vb: e.matmul(
                bank32(ps, vb)[:, 0:N],
                T["v1w"][:, DFF * k + P * f: DFF * k + P * (f + 1)],
                T["pfm"][:, CW * k: CW * k + N],
                start=(k == 0), stop=(k == NRC - 1)), incs=[("eT", 1)])
        v1_done[f] = E.now("eT")
        if f >= 1:
            E.emit("tensor", None, waits=[("eS", gelu_done[f - 1])])
            for m in range(NRC):
                E.emit("tensor", lambda e, f=f, m=m: e.matmul(
                    bank32(ps, 6 + m)[:, 0:N],
                    T["u2w"][:, R * (f - 1) + P * m: R * (f - 1) + P * (m + 1)],
                    T["hh"][:, CW * ((f - 1) % 4): CW * ((f - 1) % 4) + N],
                    start=(f - 1 == 0), stop=False), incs=[("eT", 1)])
            u2_done[f - 1] = E.now("eT")
        sw = [("eT", v1_done[f])]
        if f >= 4:
            sw.append(("eT", u2_done.get(f - 4, 0)))
        ev = E.emit("scalar", lambda e, f=f, pp=pp, vb=vb: e.activation(
            T["hh"][:, CW * pp: CW * pp + N], bank32(ps, vb)[:, 0:N],
            AF.Gelu_apprx_tanh, bias=T["b1c"][:, f:f + 1]),
            waits=sw, incs=[("eS", 1)])
        gelu_done[f] = ev["eS"]
    f = NFC - 1
    E.emit("tensor", None, waits=[("eS", gelu_done[f])])
    for m in range(NRC):
        E.emit("tensor", lambda e, f=f, m=m: e.matmul(
            bank32(ps, 6 + m)[:, 0:N],
            T["u2w"][:, R * f + P * m: R * f + P * (m + 1)],
            T["hh"][:, CW * (f % 4): CW * (f % 4) + N],
            start=False, stop=True), incs=[("eT", 1)])
    u2_done[f] = E.now("eT")
    ev_u2 = E.now("eT")
    for m in range(NRC):
        E.emit("scalar", lambda e, m=m: e.copy(
            T["qfm"][:, CW * m: CW * m + N], bank32(ps, 6 + m)[:, 0:N]),
            waits=[("eT", ev_u2)], incs=[("eS", 1)])
    ev_qfm = E.now("eS")
    evac_done = {}
    for m in range(NMC):
        vb = 2 + (m % 2)
        tw = [("eS", ev_qfm)]
        if m >= 2:
            tw.append(("eS", evac_done[m - 2]))
        E.emit("tensor", None, waits=tw)
        for k in range(NRC):
            E.emit("tensor", lambda e, m=m, k=k, vb=vb: e.matmul(
                bank32(ps, vb)[:, 0:N],
                T["v2w"][:, DM * k + P * m: DM * k + P * (m + 1)],
                T["qfm"][:, CW * k: CW * k + N],
                start=(k == 0), stop=(k == NRC - 1)), incs=[("eT", 1)])
        ev_v2 = E.now("eT")
        ev = E.emit("scalar", lambda e, m=m, vb=vb: e.activation(
            T["scr32"][:, CW * m: CW * m + N], bank32(ps, vb)[:, 0:N],
            AF.Identity, bias=T["b2h"][:, m:m + 1], scale=T["halfc"]),
            waits=[("eT", ev_v2), ("eV", T["ev_b2h"])], incs=[("eS", 1)])
        evac_done[m] = ev["eS"]


def emit_attn(E, T, t):
    C, V, bw = tile_dims(t)
    ps = T["ps"]
    ipmm, cvmm = T["ipmm"], T["cvmm"]
    hevac, sgev, gstt, gmm = {}, {}, {}, {}
    ev_s_attn0 = E.now("eS")
    E.emit("scalar", lambda e: e.preload_act_table(AF.Silu))
    ngroups = G if ATTN_GROUPS is None else ATTN_GROUPS
    for g in range(ngroups):
        pp = g % 2
        key = (t, g)
        pkey = (t, g - 2) if g >= 2 else (t - 1, g + G - 2)
        sem = "dIP%d" % pp
        ev = E.emit("sync", lambda e, g=g, pp=pp: e.dma_start(
            out=T["wip"][:, 4096 * pp: 4096 * (pp + 1)],
            in_=T["s_ip"][:, 4096 * g: 4096 * (g + 1)]),
            waits=[T["ev_s_ip"], ("eT", ipmm.get(pkey, 0))],
            incs=[(sem, 16)])
        ip_rdy = (sem, ev[sem])
        sem = "dCV%d" % pp
        ev = E.emit("sync", lambda e, g=g, pp=pp: e.dma_start(
            out=T["wcv"][:, 2048 * pp: 2048 * (pp + 1)],
            in_=T["s_cv"][:, 2048 * g: 2048 * (g + 1)]),
            waits=[T["ev_s_cv"], ("eT", cvmm.get(pkey, 0))],
            incs=[(sem, 16)])
        cv_rdy = (sem, ev[sem])
        E.emit("tensor", None, waits=[ip_rdy, ("eV", T["ev_xnfm"]),
                                      ("eS", hevac.get(g - 1, ev_s_attn0))])
        for gi in range(2):
            for k in range(NMC):
                E.emit("tensor", lambda e, gi=gi, k=k, pp=pp: e.matmul(
                    bank32(ps, gi)[:, 0:C],
                    T["wip"][:, 4096 * pp + 512 * k + P * gi:
                             4096 * pp + 512 * k + P * (gi + 1)],
                    T["xnfm"][:, CW * k: CW * k + C],
                    start=(k == 0), stop=(k == NMC - 1)), incs=[("eT", 1)])
        hmm_done = E.now("eT")
        E.emit("scalar", None, waits=[("eT", hmm_done), ("eT", cvmm.get(pkey, 0))])
        for gi in range(2):
            E.emit("scalar", lambda e, gi=gi, pp=pp: e.copy(
                T["hgrp"][:, CW * (2 * pp + gi): CW * (2 * pp + gi) + C],
                bank32(ps, gi)[:, 0:C]), incs=[("eS", 1)])
        hevac[g] = E.now("eS")
        if t == 0:
            for gi in range(2):
                E.emit("vector", lambda e, gi=gi, pp=pp: e.tensor_mul(
                    T["hgrp"][:, CW * (2 * pp + gi): CW * (2 * pp + gi) + 3],
                    T["hgrp"][:, CW * (2 * pp + gi): CW * (2 * pp + gi) + 3],
                    T["maskb"][:, 0:3]),
                    waits=[("eS", hevac[g]), ("dMK", T["ev_mask"])], incs=[("eV", 1)])
            T["ev_hmask%d" % g] = E.now("eV")
        E.emit("tensor", None, waits=[("eS", sgev.get(g - 1, 0))])
        for gi in range(2):
            for k in range(NMC):
                E.emit("tensor", lambda e, gi=gi, k=k, pp=pp: e.matmul(
                    bank32(ps, 2 + gi)[:, 0:V],
                    T["wip"][:, 4096 * pp + 512 * k + 256 + P * gi:
                             4096 * pp + 512 * k + 256 + P * (gi + 1)],
                    T["xnfm"][:, CW * k + 3: CW * k + 3 + V],
                    start=(k == 0), stop=(k == NMC - 1)), incs=[("eT", 1)])
        gmm[g] = E.now("eT")
        ipmm[key] = gmm[g]
        E.emit("scalar", None, waits=[("eT", gmm[g]), ("eV", gstt.get(g - 1, 0))])
        for gi in range(2):
            E.emit("scalar", lambda e, gi=gi: e.activation(
                T["sgrp"][:, CW * gi: CW * gi + V], bank32(ps, 2 + gi)[:, 0:V],
                AF.Silu), incs=[("eS", 1)])
        sgev[g] = E.now("eS")
        cw = [cv_rdy, ("eS", hevac[g]), ("eV", gstt.get(g - 1, 0))]
        if t == 0:
            cw.append(("eV", T["ev_hmask%d" % g]))
        E.emit("tensor", None, waits=cw)
        for oc in range(2):
            idx = 0
            for k in range(KW):
                for ci in range(2):
                    E.emit("tensor", lambda e, oc=oc, k=k, ci=ci, pp=pp,
                           idx=idx: e.matmul(
                        bank32(ps, 4 + oc)[:, 0:V],
                        T["wcv"][:, 2048 * pp + 256 * (2 * k + ci) + P * oc:
                                 2048 * pp + 256 * (2 * k + ci) + P * (oc + 1)],
                        T["hgrp"][:, CW * (2 * pp + ci) + k:
                                  CW * (2 * pp + ci) + k + V],
                        start=(idx == 0), stop=(idx == 7)), incs=[("eT", 1)])
                    idx += 1
        cvmm[key] = E.now("eT")
        E.emit("vector", None, waits=[("eT", cvmm[key]), ("eS", sgev[g])])
        for oc in range(2):
            ch = 2 * g + oc
            E.emit("vector", lambda e, oc=oc, ch=ch: e.scalar_tensor_tensor(
                T["hgt"][:, CW * ch + 3: CW * ch + 3 + V],
                bank32(ps, 4 + oc)[:, 0:V],
                T["cbc"][:, ch:ch + 1], T["sgrp"][:, CW * oc: CW * oc + V],
                ALU.add, ALU.mult), incs=[("eV", 1)])
        gstt[g] = E.now("eV")
    if ATTN_GROUPS is not None:
        return
    sqd = {}
    sqmm = {}
    E.emit("tensor", None, waits=[("eV", gstt[0])])
    for ch in range(NIC):
        pp = ch % 2
        w = [("eV", gstt[ch // 2])]
        if ch >= 2:
            w.append(("eT", sqmm[ch - 2]))
        ev = E.emit("vector", lambda e, ch=ch, pp=pp: e.tensor_mul(
            T["sqpp"][:, CW * pp: CW * pp + V],
            T["hgt"][:, CW * ch + 3: CW * ch + 3 + V],
            T["hgt"][:, CW * ch + 3: CW * ch + 3 + V]),
            waits=w, incs=[("eV", 1)])
        sqd[ch] = ev["eV"]
        E.emit("tensor", lambda e, ch=ch: e.matmul(
            bank32(ps, 6)[0:1, 0:V], T["onesb"],
            T["hgt"][:, CW * ch + 3: CW * ch + 3 + V],
            start=(ch == 0), stop=(ch == NIC - 1)),
            waits=[("eV", gstt[ch // 2])], incs=[("eT", 1)])
        ev = E.emit("tensor", lambda e, ch=ch, pp=pp: e.matmul(
            bank32(ps, 7)[0:1, 0:V], T["onesb"],
            T["sqpp"][:, CW * pp: CW * pp + V],
            start=(ch == 0), stop=(ch == NIC - 1)),
            waits=[("eV", sqd[ch])], incs=[("eT", 1)])
        sqmm[ch] = ev["eT"]
    ev_stats = E.now("eT")
    E.emit("vector", lambda e: e.tensor_copy(
        T["row_sums"][0:1, 0:V], bank32(ps, 6)[0:1, 0:V]),
        waits=[("eT", ev_stats)], incs=[("eV", 1)])
    E.emit("vector", lambda e: e.tensor_copy(
        T["row_sums"][0:1, CW:CW + V], bank32(ps, 7)[0:1, 0:V]),
        incs=[("eV", 1)])
    ev_rows = vflush(E)
    jw = [min(P, V - P * jj) for jj in range(4)]
    E.emit("tensor", None, waits=[("eV", ev_rows)])
    for jj in range(4):
        E.emit("tensor", lambda e, jj=jj, w=jw[jj]: e.transpose(
            bank32(ps, 0)[0:w, jj:jj + 1],
            T["row_sums"][0:1, P * jj: P * jj + w], T["idf"][0:1, 0:1]),
            incs=[("eT", 1)])
        E.emit("tensor", lambda e, jj=jj, w=jw[jj]: e.transpose(
            bank32(ps, 0)[0:w, 4 + jj: 5 + jj],
            T["row_sums"][0:1, CW + P * jj: CW + P * jj + w], T["idf"][0:1, 0:1]),
            incs=[("eT", 1)])
    ev_tT = E.now("eT")
    E.emit("vector", lambda e: e.tensor_copy(T["lnT"], bank32(ps, 0)[:, 0:8]),
           waits=[("eT", ev_tT)], incs=[("eV", 1)])
    E.emit("vector", lambda e: e.tensor_scalar_mul(T["st_mu"], T["lnT"][:, 0:4], 1.0 / DI),
           waits=[("eV", E.now("eV"))], incs=[("eV", 1)])
    E.emit("vector", lambda e: e.tensor_mul(T["st_m2"], T["st_mu"], T["st_mu"]),
           waits=[("eV", E.now("eV"))], incs=[("eV", 1)])
    E.emit("vector", lambda e: e.scalar_tensor_tensor(
        T["st_var"], T["lnT"][:, 4:8], 1.0 / DI, T["st_m2"], ALU.mult, ALU.subtract),
        waits=[("eV", E.now("eV"))], incs=[("eV", 1)])
    E.emit("vector", lambda e: e.tensor_scalar(
        T["st_var"], T["st_var"], EPS, None, ALU.add),
        waits=[("eV", E.now("eV"))], incs=[("eV", 1)])
    E.emit("scalar", lambda e: e.preload_act_table(AF.Silu))
    emit_rsqrt(E, T, T["st_rs"], T["st_var"], T["st_sd"])
    E.emit("vector", lambda e: e.tensor_mul(T["st_am"], T["st_mu"], T["st_rs"]),
           waits=[("eV", E.now("eV"))], incs=[("eV", 1)])
    ev_ra = vflush(E)
    E.emit("tensor", None, waits=[("eV", ev_ra)])
    for jj in range(4):
        E.emit("tensor", lambda e, jj=jj, w=jw[jj]: e.transpose(
            bank32(ps, 1)[0:1, P * jj: P * jj + w],
            T["st_rs"][0:w, jj:jj + 1], T["idf"][0:w, 0:w]),
            incs=[("eT", 1)])
    ev_t1 = E.now("eT")
    E.emit("vector", lambda e: e.tensor_copy(
        T["row_ra"][0:1, 0:V], bank32(ps, 1)[0:1, 0:V]),
        waits=[("eT", ev_t1)], incs=[("eV", 1)])
    ev_e1 = vflush(E)
    E.emit("tensor", None, waits=[("eV", ev_e1)])
    for jj in range(4):
        E.emit("tensor", lambda e, jj=jj, w=jw[jj]: e.transpose(
            bank32(ps, 1)[0:1, P * jj: P * jj + w],
            T["st_am"][0:w, jj:jj + 1], T["idf"][0:w, 0:w]),
            incs=[("eT", 1)])
    ev_t2 = E.now("eT")
    E.emit("vector", lambda e: e.tensor_copy(
        T["row_ra"][0:1, CW:CW + V], bank32(ps, 1)[0:1, 0:V]),
        waits=[("eT", ev_t2)], incs=[("eV", 1)])
    ev_e2 = vflush(E)
    E.emit("tensor", None, waits=[("eV", ev_e2)])
    E.emit("tensor", lambda e: e.matmul(
        bank32(ps, 2)[:, 0:V], T["onesr"], T["row_ra"][0:1, 0:V],
        start=True, stop=True), incs=[("eT", 1)])
    E.emit("tensor", lambda e: e.matmul(
        bank32(ps, 3)[:, 0:V], T["onesr"], T["row_ra"][0:1, CW:CW + V],
        start=True, stop=True), incs=[("eT", 1)])
    ev_bc = E.now("eT")
    apl = {}
    for ch in range(NIC):
        tmp = T["scr32"][:, (6 + ch % 2) * CW: (6 + ch % 2) * CW + V]
        w = [("eT", ev_bc)]
        if ch >= 2:
            w.append(("eS", apl[ch - 2]))
        E.emit("vector", lambda e, ch=ch, tmp=tmp: e.tensor_mul(
            tmp, T["hgt"][:, CW * ch + 3: CW * ch + 3 + V], bank32(ps, 2)[:, 0:V]),
            waits=w, incs=[("eV", 1)])
        ev = E.emit("vector", lambda e, tmp=tmp: e.tensor_sub(
            tmp, tmp, bank32(ps, 3)[:, 0:V]), incs=[("eV", 1)])
        ev2 = E.emit("scalar", lambda e, ch=ch, tmp=tmp: e.activation(
            T["hnb"][:, CW * ch + 3: CW * ch + 3 + V], tmp,
            AF.Identity, bias=T["lbc"][:, ch:ch + 1], scale=T["lgc"][:, ch:ch + 1]),
            waits=[("eV", ev["eV"])], incs=[("eS", 1)])
        apl[ch] = ev2["eS"]
    ev_hn = E.now("eS")
    if ATTN_PHASE == 1:
        return
    fcmm = T["fcmm"]
    fcev = {}
    for m in range(NMC):
        pp = m % 2
        pkey = (t, m - 2) if m >= 2 else (t - 1, m + NMC - 2)
        sem = "dFC%d" % pp
        ev = E.emit("sync", lambda e, m=m, pp=pp: e.dma_start(
            out=T["wfc"][:, 2048 * pp: 2048 * (pp + 1)],
            in_=T["s_fc"][:, 2048 * m: 2048 * (m + 1)]),
            waits=[T["ev_s_fc"], ("eT", fcmm.get(pkey, 0))],
            incs=[(sem, 16)])
        tw = [(sem, ev[sem]), ("eS", ev_hn)]
        if m >= 2:
            tw.append(("eS", fcev[m - 2]))
        E.emit("tensor", None, waits=tw)
        for k in range(NIC):
            E.emit("tensor", lambda e, m=m, k=k, pp=pp: e.matmul(
                bank32(ps, 4 + (m % 2))[:, 0:C],
                T["wfc"][:, 2048 * pp + P * k: 2048 * pp + P * (k + 1)],
                T["hnb"][:, CW * k: CW * k + C],
                start=(k == 0), stop=(k == NIC - 1)), incs=[("eT", 1)])
        fcmm[(t, m)] = E.now("eT")
        ev = E.emit("scalar", lambda e, m=m: e.activation(
            T["sfc"][:, CW * m: CW * m + C], bank32(ps, 4 + (m % 2))[:, 0:C],
            AF.Silu, bias=T["fbc"][:, m:m + 1]),
            waits=[("eT", fcmm[(t, m)])], incs=[("eS", 1)])
        fcev[m] = ev["eS"]
    if ATTN_PHASE == 2:
        return
    cpmm = T["cpmm"]
    cpev = {}
    for m in range(NIC):
        pp = m % 2
        pkey = (t, m - 2) if m >= 2 else (t - 1, m + NIC - 2)
        sem = "dCP%d" % pp
        ev = E.emit("sync", lambda e, m=m, pp=pp: e.dma_start(
            out=T["wcp"][:, 1024 * pp: 1024 * (pp + 1)],
            in_=T["s_cp"][:, 1024 * m: 1024 * (m + 1)]),
            waits=[T["ev_s_cp"], ("eT", cpmm.get(pkey, 0))],
            incs=[(sem, 16)])
        tw = [(sem, ev[sem]), ("eS", fcev[NMC - 1])]
        if m >= 2:
            tw.append(("eV", cpev[m - 2]))
        E.emit("tensor", None, waits=tw)
        for k in range(NMC):
            E.emit("tensor", lambda e, m=m, k=k, pp=pp: e.matmul(
                bank32(ps, 6 + (m % 2))[:, 0:C],
                T["wcp"][:, 1024 * pp + P * k: 1024 * pp + P * (k + 1)],
                T["sfc"][:, CW * k: CW * k + C],
                start=(k == 0), stop=(k == NMC - 1)), incs=[("eT", 1)])
        cpmm[(t, m)] = E.now("eT")
        ev = E.emit("vector", lambda e, m=m: e.scalar_tensor_tensor(
            T["hnb"][:, CW * m: CW * m + C], bank32(ps, 6 + (m % 2))[:, 0:C],
            T["cpc"][:, m:m + 1], T["hgt"][:, CW * m: CW * m + C],
            ALU.add, ALU.add),
            waits=[("eT", cpmm[(t, m)])], incs=[("eV", 1)])
        cpev[m] = ev["eV"]
    ev_h2 = E.now("eV")
    if ATTN_PHASE == 3:
        return
    owmm = T["owmm"]
    owev = {}
    for m in range(NMC):
        pp = m % 2
        pkey = (t, m - 2) if m >= 2 else (t - 1, m + NMC - 2)
        sem = "dOW%d" % pp
        ev = E.emit("sync", lambda e, m=m, pp=pp: e.dma_start(
            out=T["wow"][:, 2048 * pp: 2048 * (pp + 1)],
            in_=T["s_ow"][:, 2048 * m: 2048 * (m + 1)]),
            waits=[T["ev_s_ow"], ("eT", owmm.get(pkey, 0))],
            incs=[(sem, 16)])
        tw = [(sem, ev[sem]), ("eV", ev_h2)]
        if m >= 2:
            tw.append(("eS", owev[m - 2]))
        E.emit("tensor", None, waits=tw)
        for k in range(NIC):
            E.emit("tensor", lambda e, m=m, k=k, pp=pp: e.matmul(
                bank32(ps, (m % 2))[:, 0:C],
                T["wow"][:, 2048 * pp + P * k: 2048 * pp + P * (k + 1)],
                T["hnb"][:, CW * k: CW * k + C],
                start=(k == 0), stop=(k == NIC - 1)), incs=[("eT", 1)])
        owmm[(t, m)] = E.now("eT")
        ev = E.emit("scalar", lambda e, m=m: e.copy(
            T["scr32"][:, CW * m: CW * m + C], bank32(ps, (m % 2))[:, 0:C]),
            waits=[("eT", owmm[(t, m)])], incs=[("eS", 1)])
        owev[m] = ev["eS"]


def emit_xout(E, T, t):
    C, V, bw = tile_dims(t)
    s = STARTS[t]
    acc = 0
    ev = None
    for j in range(4):
        rlo = 3 if j == 0 else 0
        w = (bw[j] - 3) if j == 0 else bw[j]
        if w <= 0:
            continue
        ev = E.emit("gpsimd", lambda e, j=j, rlo=rlo, w=w, o=acc, s=s: e.dma_start(
            out=T["d_out"][s + o: s + o + w, :],
            in_=T["xB"][rlo: rlo + w, DM * j: DM * (j + 1)]),
            waits=[("eV", T["ev_stream"])] if j == 0 else (),
            incs=[("dOU", 16)])
        acc += w
    assert acc == V
    T["ev_out_dma"] = ev["dOU"]


def emit_tile(E, T, t):
    if not T.pop("t1_done", False):
        emit_xpose_xn(E, T, t)
    emit_ffn(E, T, t)
    emit_stream_add(E, T, t, T["xB"], T["xC"], accum=True)
    T["ev_xC_free_v"] = E.now("eV")
    emit_norm23(E, T, t, "xB", 2)
    emit_xpose_xn(E, T, t)
    emit_attn(E, T, t)
    if t + 1 < NT:
        emit_xin(E, T, t + 1)
        emit_norm1_stats(E, T, t + 1)
    emit_stream_add(E, T, t, T["xA"], T["xB"], accum=True)
    emit_norm23(E, T, t, "xA", 3)
    emit_xpose_xn(E, T, t)

    def post_u1():
        if t + 1 < NT:
            emit_norm1_apply(E, T, t + 1)
            emit_xpose_xn(E, T, t + 1)
            T["t1_done"] = True

    emit_ffn(E, T, t, post_u1=post_u1)
    emit_stream_add(E, T, t, T["xB"], T["xA"], accum=False)
    emit_xout(E, T, t)



def _chunk_cols(v, n):
    return np.ascontiguousarray(v.reshape(n, P).T).astype(np.float32)


def stage_inputs(inp):
    x = np.asarray(inp["L0_attn_hidden_states"], dtype=np.float32)
    ipw = np.asarray(inp["L0_attn_in_proj_w"], dtype=np.float32)
    cvw = np.asarray(inp["L0_attn_conv_w"], dtype=np.float32)
    fcw = np.asarray(inp["L0_attn_fc_w"], dtype=np.float32)
    cpw = np.asarray(inp["L0_attn_cproj_w"], dtype=np.float32)
    oww = np.asarray(inp["L0_attn_out_w"], dtype=np.float32)

    t1 = ipw.T.reshape(NMC, P, 32, P)
    gidx = np.array([[2 * g, 2 * g + 1, 16 + 2 * g, 17 + 2 * g] for g in range(G)])
    iph = t1[:, :, gidx.reshape(-1), :].reshape(NMC, P, G, 4, P)
    iph = iph.transpose(1, 2, 0, 3, 4).reshape(P, G * 4096)

    a = cvw.transpose(2, 1, 0).reshape(KW, 2, P, 16, P)
    cvh = a.transpose(2, 0, 1, 3, 4)
    cvh = np.stack([cvh[:, :, :, 2 * g: 2 * g + 2, :].reshape(P, 2048)
                    for g in range(G)], axis=1).reshape(P, G * 2048)

    fch = fcw.T.reshape(NIC, P, NMC, P).transpose(1, 2, 0, 3).reshape(P, NMC * 2048)
    cph = cpw.T.reshape(NMC, P, NIC, P).transpose(1, 2, 0, 3).reshape(P, NIC * 1024)
    owh = oww.T.reshape(NIC, P, NMC, P).transpose(1, 2, 0, 3).reshape(P, NMC * 2048)

    u1h = np.asarray(inp["L0_ffn_U1"], np.float32).reshape(NMC, P, R) \
        .transpose(1, 0, 2).reshape(P, NMC * R)
    v1h = np.asarray(inp["L0_ffn_V1"], np.float32).reshape(NRC, P, DFF) \
        .transpose(1, 0, 2).reshape(P, NRC * DFF)
    u2h = np.asarray(inp["L0_ffn_U2"], np.float32).reshape(NFC, P, R) \
        .transpose(1, 0, 2).reshape(P, NFC * R)
    v2h = np.asarray(inp["L0_ffn_V2"], np.float32).reshape(NRC, P, DM) \
        .transpose(1, 0, 2).reshape(P, NRC * DM)

    common = {
        "ip_in": np.ascontiguousarray(iph), "cv_in": np.ascontiguousarray(cvh),
        "fc_in": np.ascontiguousarray(fch), "cp_in": np.ascontiguousarray(cph),
        "ow_in": np.ascontiguousarray(owh),
        "u1_in": np.ascontiguousarray(u1h), "v1_in": np.ascontiguousarray(v1h),
        "u2_in": np.ascontiguousarray(u2h), "v2_in": np.ascontiguousarray(v2h),
        "b1_in": _chunk_cols(np.asarray(inp["L0_ffn_b1"], np.float32), NFC),
        "b2_in": _chunk_cols(np.asarray(inp["L0_ffn_b2"], np.float32), NMC),
        "cb_in": _chunk_cols(np.asarray(inp["L0_attn_conv_b"], np.float32), NIC),
        "fb_in": _chunk_cols(np.asarray(inp["L0_attn_fc_b"], np.float32), NMC),
        "cpb_in": _chunk_cols(np.asarray(inp["L0_attn_cproj_b"], np.float32), NIC),
        "lg_in": _chunk_cols(np.asarray(inp["L0_attn_ln_g"], np.float32), NIC),
        "lb_in": _chunk_cols(np.asarray(inp["L0_attn_ln_b"], np.float32), NIC),
    }
    in_maps = []
    for c in range(NCORES):
        b, h = c // 2, c % 2
        lo = 2048 * h
        if h == 0:
            halo = np.zeros((3, DM), dtype=np.float32)
            mask = np.zeros((P, 4), dtype=np.float32)
        else:
            halo = x[b, lo - 3: lo]
            mask = np.ones((P, 4), dtype=np.float32)
        xi = np.ascontiguousarray(np.concatenate([halo, x[b, lo: lo + 2048]], axis=0))
        m = dict(common)
        m["x_in"] = xi
        m["mask_in"] = mask
        in_maps.append(m)
    return in_maps


def kernel(**inputs):
    in_maps = stage_inputs(inputs)
    nc = build_nc()
    res = run_bass_kernel_spmd(nc, in_maps, core_ids=list(range(NCORES)))
    out = np.empty((4, 4096, DM), dtype=np.float32)
    for c in range(NCORES):
        b, h = c // 2, c % 2
        out[b, 2048 * h: 2048 * (h + 1)] = np.asarray(res.results[c]["x_out"])
    return out


# version 06 (final)
# speedup vs baseline: 1.0572x; 1.0572x over previous
import numpy as np
from contextlib import ExitStack

import concourse.bass as bass
import concourse.mybir as mybir
from concourse.bass_utils import run_bass_kernel_spmd

F32 = mybir.dt.float32
BF16 = mybir.dt.bfloat16
AF = mybir.ActivationFunctionType
ALU = mybir.AluOpType
AX = mybir.AxisListType

P = 128
DM, DI, DFF, R, KW, G = 1024, 2048, 4096, 256, 4, 8
NMC, NIC, NFC, NRC = DM // P, DI // P, DFF // P, R // P
EPS = 1e-5
NCORES = 8
TOK = 2048
OWN = [410, 410, 410, 409, 409]
STARTS = [0, 410, 820, 1230, 1639]
NT = len(OWN)
CW = 416
MAGIC = 0x5F3759DF
I32 = mybir.dt.int32

ENGS = ("tensor", "vector", "scalar", "sync", "gpsimd")
DEBUG = False
STOP_AFTER = None
ATTN_GROUPS = None
ATTN_PHASE = None


class Em:

    def __init__(self):
        self.q = {e: [] for e in ENGS}
        self.cnt = {}
        self.waited = {}

    def now(self, s):
        return self.cnt.setdefault(s, 0)

    def emit(self, eng, fn=None, waits=(), incs=()):
        wl = []
        for s, v in waits:
            self.now(s)
            if v is None:
                v = self.cnt[s]
            if v > 0 and v > self.waited.get((eng, s), 0):
                assert v <= self.cnt[s], f"wait {s}>={v} exceeds emitted {self.cnt[s]}"
                self.waited[(eng, s)] = v
                wl.append((s, v))
        out = {}
        il = []
        for s, k in incs:
            self.cnt[s] = self.now(s) + k
            out[s] = self.cnt[s]
            il.append((s, k))
        if fn is None and not il and not wl:
            return out
        self.q[eng].append((fn, wl, il))
        return out


def bank32(ps, b, n=1):
    return ps.ap()[:, 512 * b: 512 * (b + n)]


def bank16(ps, b, n=1):
    return ps.ap().bitcast(BF16)[:, 1024 * b: 1024 * (b + n)]


def tile_dims(t):
    V = OWN[t]
    C = V + 3
    bw = [min(P, C - P * j) for j in range(4)]
    return C, V, bw


def build_nc():
    nc = bass.Bass("TRN2", target_bir_lowering=False)

    d_x = nc.dram_tensor("x_in", [TOK + 3, DM], F32, kind="ExternalInput").ap()
    d_mask = nc.dram_tensor("mask_in", [P, 4], F32, kind="ExternalInput").ap()
    d_u1 = nc.dram_tensor("u1_in", [P, NMC * R], F32, kind="ExternalInput").ap()
    d_v1 = nc.dram_tensor("v1_in", [P, NRC * DFF], F32, kind="ExternalInput").ap()
    d_u2 = nc.dram_tensor("u2_in", [P, NFC * R], F32, kind="ExternalInput").ap()
    d_v2 = nc.dram_tensor("v2_in", [P, NRC * DM], F32, kind="ExternalInput").ap()
    d_ip = nc.dram_tensor("ip_in", [P, G * 4096], F32, kind="ExternalInput").ap()
    d_cv = nc.dram_tensor("cv_in", [P, G * 2048], F32, kind="ExternalInput").ap()
    d_fc = nc.dram_tensor("fc_in", [P, NMC * 2048], F32, kind="ExternalInput").ap()
    d_cp = nc.dram_tensor("cp_in", [P, NIC * 1024], F32, kind="ExternalInput").ap()
    d_ow = nc.dram_tensor("ow_in", [P, NMC * 2048], F32, kind="ExternalInput").ap()
    d_b1 = nc.dram_tensor("b1_in", [P, NFC], F32, kind="ExternalInput").ap()
    d_b2 = nc.dram_tensor("b2_in", [P, NMC], F32, kind="ExternalInput").ap()
    d_cb = nc.dram_tensor("cb_in", [P, NIC], F32, kind="ExternalInput").ap()
    d_fb = nc.dram_tensor("fb_in", [P, NMC], F32, kind="ExternalInput").ap()
    d_cpb = nc.dram_tensor("cpb_in", [P, NIC], F32, kind="ExternalInput").ap()
    d_lg = nc.dram_tensor("lg_in", [P, NIC], F32, kind="ExternalInput").ap()
    d_lb = nc.dram_tensor("lb_in", [P, NIC], F32, kind="ExternalInput").ap()

    d_out = nc.dram_tensor("x_out", [TOK, DM], F32, kind="ExternalOutput").ap()

    s_ip = nc.dram_tensor("s_ip", [P, G * 4096], BF16, kind="Internal").ap()
    s_cv = nc.dram_tensor("s_cv", [P, G * 2048], BF16, kind="Internal").ap()
    s_fc = nc.dram_tensor("s_fc", [P, NMC * 2048], BF16, kind="Internal").ap()
    s_cp = nc.dram_tensor("s_cp", [P, NIC * 1024], BF16, kind="Internal").ap()
    s_ow = nc.dram_tensor("s_ow", [P, NMC * 2048], BF16, kind="Internal").ap()

    A = nc.alloc_sbuf_tensor
    xA = A("xA", [P, 4 * DM], F32).ap()
    xB = A("xB", [P, 4 * DM], F32).ap()
    xntm = A("xntm", [P, 4 * DM], BF16).ap()
    sfc = xntm[:, 0:NMC * CW]
    xnfm = A("xnfm", [P, NMC * CW], BF16).ap()
    scr32 = A("scr32", [P, NMC * CW], F32).ap()
    pfm = A("pfm", [P, NRC * CW], BF16).ap()
    qfm = A("qfm", [P, NRC * CW], BF16).ap()
    hh = A("hh", [P, 4 * CW], BF16).ap()
    hgt = A("hgt", [P, NIC * CW], BF16).ap()
    hnb = A("hnb", [P, NIC * CW], BF16).ap()
    hgrp = A("hgrp", [P, 2 * 2 * CW], BF16).ap()
    sgrp = A("sgrp", [P, 2 * CW], BF16).ap()
    sqpp = A("sqpp", [P, 2 * CW], BF16).ap()

    u1w = A("u1w", [P, NMC * R], BF16).ap()
    v1w = A("v1w", [P, NRC * DFF], BF16).ap()
    u2w = A("u2w", [P, NFC * R], BF16).ap()
    v2w = A("v2w", [P, NRC * DM], BF16).ap()
    wip = A("wip", [P, 2 * 4096], BF16).ap()
    wcv = A("wcv", [P, 2 * 2048], BF16).ap()
    wfc = A("wfc", [P, 2 * 2048], BF16).ap()
    wcp = A("wcp", [P, 2 * 1024], BF16).ap()
    wow = A("wow", [P, 2 * 2048], BF16).ap()

    b1c = A("b1c", [P, NFC], F32).ap()
    b2h = A("b2h", [P, NMC], F32).ap()
    b2c = A("b2c", [P, NMC], F32).ap()
    cbc = A("cbc", [P, NIC], F32).ap()
    fbc = A("fbc", [P, NMC], F32).ap()
    cpc = A("cpc", [P, NIC], F32).ap()
    lgc = A("lgc", [P, NIC], F32).ap()
    lbc = A("lbc", [P, NIC], F32).ap()
    maskb = A("maskb", [P, 4], BF16).ap()

    idb = A("idb", [P, P], BF16).ap()
    idf = A("idf", [P, P], F32).ap()
    onesb = A("onesb", [P, 1], BF16).ap()
    onesr = A("onesr", [1, P], F32).ap()
    epsb = A("epsb", [P, 1], F32).ap()
    halfc = A("halfc", [P, 1], F32).ap()

    xC = A("xC", [P, 4 * DM], F32).ap()
    bn6 = A("bn6", [P, 48], F32).ap()
    bnsq = A("bnsq", [P, 16], F32).ap()
    st2_sm = A("st2_sm", [P, 4], F32).ap()
    st2_sq = A("st2_sq", [P, 4], F32).ap()
    st2_mu = A("st2_mu", [P, 4], F32).ap()
    st2_var = A("st2_var", [P, 4], F32).ap()
    st2_sd = A("st2_sd", [P, 4], F32).ap()
    st2_rs = A("st2_rs", [P, 4], F32).ap()
    st2_am = A("st2_am", [P, 4], F32).ap()
    st_s1 = A("st_s1", [P, 4], F32).ap()
    st_s2 = A("st_s2", [P, 4], F32).ap()
    st_sm = st_s1
    st_sq = st_s2
    st_mu = A("st_mu", [P, 4], F32).ap()
    st_m2 = A("st_m2", [P, 4], F32).ap()
    st_var = A("st_var", [P, 4], F32).ap()
    st_sd = A("st_sd", [P, 4], F32).ap()
    st_rs = A("st_rs", [P, 4], F32).ap()
    st_am = A("st_am", [P, 4], F32).ap()
    lnT = A("lnT", [P, 8], F32).ap()
    row_sums = A("row_sums", [1, 2 * CW], F32).ap()
    row_ra = A("row_ra", [1, 2 * CW], F32).ap()

    ps = nc.alloc_psum_tensor("ps", [P, 4096], F32)

    T = dict(locals())
    T["ipmm"], T["cvmm"], T["fcmm"], T["cpmm"], T["owmm"] = {}, {}, {}, {}, {}

    E = Em()
    emit_prologue(E, T)
    for t in range(NT):
        emit_tile(E, T, t)
    if DEBUG:
        for name in ("xA", "xB", "hgt", "hnb", "xnfm", "scr32", "st_s1", "st_s2",
                     "st_rs", "st_am", "lnT", "row_sums", "row_ra", "xntm", "sgrp",
                     "hgrp", "pfm", "qfm", "st_mu", "st_m2", "st_var", "st_sd",
                     "epsb", "halfc", "xC", "bn6", "st2_rs", "st2_am", "st2_mu",
                     "st2_var"):
            ap = T[name]
            sh = list(ap.shape)
            dbg = nc.dram_tensor("dbg_" + name, sh, F32, kind="ExternalOutput").ap()
            E.emit("gpsimd", lambda e, ap=ap, dbg=dbg: e.dma_start(out=dbg, in_=ap),
                   waits=[("eV", E.now("eV")), ("eS", E.now("eS")),
                          ("eT", E.now("eT"))],
                   incs=[("dOU", 16)])
    E.emit("gpsimd", None, waits=[("dOU", E.now("dOU"))])

    with ExitStack() as st:
        sems = {s: st.enter_context(nc.semaphore(s)) for s in sorted(E.cnt)}
        block = st.enter_context(nc.Block())
        for ename in ENGS:
            items = E.q[ename]
            if not items:
                continue

            def runner(eng, items=items):
                for fn, wl, il in items:
                    for s, v in wl:
                        eng.wait_ge(sems[s], v)
                    if fn is not None:
                        inst = fn(eng)
                        for s, k in il:
                            inst.then_inc(sems[s], k)

            getattr(block, ename)(runner)
    return nc



def emit_prologue(E, T):
    g = "gpsimd"
    for name in ("xA", "xB", "xntm", "xnfm", "scr32", "pfm", "qfm", "hh",
                 "hgt", "hnb", "hgrp", "sgrp", "sqpp", "row_sums", "row_ra",
                 "lnT", "st_s1", "st_s2", "st_mu", "st_m2", "st_var", "st_sd",
                 "st_rs", "st_am", "xC", "bn6", "bnsq", "st2_sm", "st2_sq", "st2_mu",
                 "st2_var", "st2_sd", "st2_rs", "st2_am", "idb", "idf"):
        E.emit(g, lambda e, n=name: e.memset(T[n], 0.0), incs=[("eG", 1)])
    E.emit(g, lambda e: e.affine_select(
        out=T["idb"], in_=T["idb"], compare_op=ALU.not_equal, fill=1.0,
        base=0, pattern=[[-1, P]], channel_multiplier=1), incs=[("eG", 1)])
    E.emit(g, lambda e: e.affine_select(
        out=T["idf"], in_=T["idf"], compare_op=ALU.not_equal, fill=1.0,
        base=0, pattern=[[-1, P]], channel_multiplier=1), incs=[("eG", 1)])
    E.emit(g, lambda e: e.memset(T["onesb"], 1.0), incs=[("eG", 1)])
    E.emit(g, lambda e: e.memset(T["onesr"], 1.0), incs=[("eG", 1)])
    E.emit(g, lambda e: e.memset(T["epsb"], EPS), incs=[("eG", 1)])
    E.emit(g, lambda e: e.memset(T["halfc"], 0.5), incs=[("eG", 1)])
    T["ev_setup"] = E.now("eG")
    emit_xin(E, T, 0)
    emit_norm1_stats(E, T, 0)
    emit_norm1_apply(E, T, 0, waits_eT=0)
    E.emit(g, lambda e: e.dma_start(out=T["maskb"], in_=T["d_mask"]), incs=[("dMK", 16)])
    T["ev_mask"] = E.now("dMK")
    for dst, src in (("u1w", "d_u1"), ("v1w", "d_v1"), ("u2w", "d_u2"), ("v2w", "d_v2")):
        E.emit(g, lambda e, d=dst, s=src: e.dma_start(out=T[d], in_=T[s]),
               incs=[("dFW", 16)])
    T["ev_ffnw"] = E.now("dFW")
    for dst, src, sem in (("s_ip", "d_ip", "dSIP"), ("s_cv", "d_cv", "dSCV"),
                          ("s_fc", "d_fc", "dSFC"), ("s_cp", "d_cp", "dSCP"),
                          ("s_ow", "d_ow", "dSOW")):
        ev = E.emit(g, lambda e, d=dst, s=src: e.dma_start(out=T[d], in_=T[s]),
                    incs=[(sem, 16)])
        T["ev_" + dst] = (sem, ev[sem])
    for dst, src in (("b1c", "d_b1"), ("b2c", "d_b2"), ("cbc", "d_cb"),
                     ("fbc", "d_fb"), ("cpc", "d_cpb"), ("lgc", "d_lg"),
                     ("lbc", "d_lb")):
        E.emit("sync", lambda e, d=dst, s=src: e.dma_start(out=T[d], in_=T[s]),
               incs=[("dBI", 16)])
    T["ev_bias"] = E.now("dBI")
    E.emit("vector", lambda e: e.tensor_scalar_mul(T["b2h"], T["b2c"], 0.5),
           waits=[("dBI", T["ev_bias"])], incs=[("eV", 1)])
    T["ev_b2h"] = E.now("eV")



def emit_rsqrt(E, T, y, v, tmp, iters=3):
    E.emit("vector", lambda e: e.tensor_scalar(
        y.bitcast(I32), v.bitcast(I32), 1, -1,
        ALU.arith_shift_right, ALU.bitwise_xor),
        waits=[("eV", E.now("eV"))], incs=[("eV", 1)])
    E.emit("vector", lambda e: e.tensor_scalar(
        y.bitcast(I32), y.bitcast(I32), MAGIC + 1, None, ALU.add),
        waits=[("eV", E.now("eV"))], incs=[("eV", 1)])
    for _ in range(iters):
        E.emit("vector", lambda e: e.tensor_mul(tmp, y, y),
               waits=[("eV", E.now("eV"))], incs=[("eV", 1)])
        E.emit("vector", lambda e: e.tensor_mul(tmp, tmp, v),
               waits=[("eV", E.now("eV"))], incs=[("eV", 1)])
        E.emit("vector", lambda e: e.tensor_scalar(tmp, tmp, -0.5, 1.5,
                                                   ALU.mult, ALU.add),
               waits=[("eV", E.now("eV"))], incs=[("eV", 1)])
        E.emit("vector", lambda e: e.tensor_mul(y, y, tmp),
               waits=[("eV", E.now("eV"))], incs=[("eV", 1)])


def vflush(E):
    ev = E.emit("vector", lambda e: e.engine_nop(),
                waits=[("eV", E.now("eV"))], incs=[("eV", 1)])
    return ev["eV"]


def emit_xin(E, T, t):
    C, V, bw = tile_dims(t)
    r0 = STARTS[t]
    waits = []
    if t > 0:
        waits = [("eV", T["ev_xC_free_v"])]
    ev = None
    for j in range(4):
        ev = E.emit("gpsimd", lambda e, j=j, r0=r0, b=bw[j]: e.dma_start(
            out=T["xC"][0:b, DM * j: DM * (j + 1)],
            in_=T["d_x"][r0 + P * j: r0 + P * j + b, :]),
            waits=waits if j == 0 else (), incs=[("dXI", 16)])
    T["ev_xin"] = ev["dXI"]


def emit_bn_chain(E, T, src, pfx, blk_waits):
    bn6 = T["bn6"]
    x = T[src]
    for j in range(4):
        for h in range(2):
            E.emit("vector", lambda e, j=j, h=h: e.bn_stats(
                bn6[:, 12 * j + 6 * h: 12 * j + 6 * h + 6],
                x[:, DM * j + 512 * h: DM * j + 512 * h + 512]),
                waits=blk_waits[j] if h == 0 else (), incs=[("eV", 1)])
    b3 = bn6.rearrange("p (j c) -> p j c", c=12)
    sm, sq = T[pfx + "_sm"], T[pfx + "_sq"]
    mu, var, sd, rs, am = (T[pfx + "_mu"], T[pfx + "_var"], T[pfx + "_sd"],
                           T[pfx + "_rs"], T[pfx + "_am"])
    m4 = b3[:, :, 1:11:3]
    cv4 = b3[:, :, 2:12:3]
    E.emit("vector", lambda e: e.reduce_sum(sm, m4, axis=AX.X),
           waits=[("eV", E.now("eV"))], incs=[("eV", 1)])
    E.emit("vector", lambda e: e.tensor_mul(T["bnsq"], m4, m4),
           incs=[("eV", 1)])
    E.emit("vector", lambda e: e.reduce_sum(var, cv4, axis=AX.X),
           incs=[("eV", 1)])
    E.emit("vector", lambda e: e.reduce_sum(
        sq, T["bnsq"].rearrange("p (j g) -> p j g", g=4), axis=AX.X),
        waits=[("eV", E.now("eV"))], incs=[("eV", 1)])
    E.emit("vector", lambda e: e.tensor_scalar_mul(mu, sm, 0.25),
           incs=[("eV", 1)])
    E.emit("vector", lambda e: e.scalar_tensor_tensor(
        sq, sq, 0.25 / DM, var, ALU.mult, ALU.bypass),
        waits=[("eV", E.now("eV"))], incs=[("eV", 1)]) if False else None
    E.emit("vector", lambda e: e.tensor_scalar(sq, sq, 0.25, None, ALU.mult),
           waits=[("eV", E.now("eV"))], incs=[("eV", 1)])
    E.emit("vector", lambda e: e.scalar_tensor_tensor(
        var, var, 1.0 / DM, sq, ALU.mult, ALU.add),
        waits=[("eV", E.now("eV"))], incs=[("eV", 1)])
    E.emit("vector", lambda e: e.tensor_mul(sq, mu, mu),
           incs=[("eV", 1)])
    E.emit("vector", lambda e: e.scalar_tensor_tensor(
        var, sq, -1.0, var, ALU.mult, ALU.add),
        waits=[("eV", E.now("eV"))], incs=[("eV", 1)])
    E.emit("vector", lambda e: e.tensor_scalar(var, var, EPS, None, ALU.add),
           waits=[("eV", E.now("eV"))], incs=[("eV", 1)])
    emit_rsqrt(E, T, rs, var, sd, iters=2)
    E.emit("vector", lambda e: e.tensor_mul(am, mu, rs),
           waits=[("eV", E.now("eV"))], incs=[("eV", 1)])


def emit_norm1_stats(E, T, t):
    w = [[("dXI", T["ev_xin"])], (), (), ()]
    emit_bn_chain(E, T, "xC", "st2", w)
    T["ev_n1chain"] = E.now("eV")


def emit_norm1_apply(E, T, t, waits_eT=None):
    if waits_eT is None:
        waits_eT = E.now("eT")
    evb = []
    for j in range(4):
        w = [("eV", T["ev_n1chain"])] if j == 0 else []
        if j == 0 and waits_eT:
            w.append(("eT", waits_eT))
        ev = E.emit("vector", lambda e, j=j: e.tensor_scalar(
            T["xntm"][:, DM * j: DM * (j + 1)], T["xC"][:, DM * j: DM * (j + 1)],
            T["st2_rs"][:, j:j + 1], T["st2_am"][:, j:j + 1],
            ALU.mult, ALU.subtract), waits=w, incs=[("eV", 1)])
        evb.append(ev["eV"])
    T["ev_apply_blk"] = evb
    T["ev_apply"] = E.now("eV")


def emit_norm23(E, T, t, src, which):
    x = T[src]
    bw_ = [[("eV", T.get("ev_stt_blk%d" % j, 0))] for j in range(4)]
    emit_bn_chain(E, T, src, "st", bw_)
    E.emit("vector", None, waits=[("eT", E.now("eT")), ("eV", E.now("eV"))])
    evb = []
    for j in range(4):
        ev = E.emit("vector", lambda e, j=j: e.tensor_scalar(
            T["xntm"][:, DM * j: DM * (j + 1)], x[:, DM * j: DM * (j + 1)],
            T["st_rs"][:, j:j + 1], T["st_am"][:, j:j + 1],
            ALU.mult, ALU.subtract), incs=[("eV", 1)])
        evb.append(ev["eV"])
    T["ev_apply_blk"] = evb
    T["ev_apply"] = E.now("eV")
    if which == 3:
        T["ev_xA_free_s"] = E.now("eS")
        T["ev_xA_free_v"] = T["ev_apply"]


def emit_xpose_xn(E, T, t):
    C, V, bw = tile_dims(t)
    evb = T["ev_apply_blk"]
    for j in range(4):
        b = 4 + (j % 2)
        w = bw[j]
        E.emit("tensor", None, waits=[("eV", T.get("ev_ptb%d" % (j % 2), 0)),
                                      ("eV", evb[j]),
                                      ("eG", T["ev_setup"])])
        for f in range(NMC):
            E.emit("tensor", lambda e, j=j, f=f, b=b: e.transpose(
                bank16(T["ps"], b)[:, P * f: P * (f + 1)],
                T["xntm"][:, DM * j + P * f: DM * j + P * (f + 1)],
                T["idb"]), incs=[("eT", 1)])
        ev_t = E.now("eT")
        ev = E.emit("vector", lambda e, j=j, b=b, w=w: e.tensor_copy(
            T["xnfm"].rearrange("p (f c) -> p f c", c=CW)[:, :, P * j: P * j + w],
            bank16(T["ps"], b).rearrange("p (f c) -> p f c", c=P)[:, :, 0:w]),
            waits=[("eT", ev_t)], incs=[("eV", 1)])
        T["ev_ptb%d" % (j % 2)] = ev["eV"]
    T["ev_xnfm"] = E.now("eV")


def emit_stream_add(E, T, t, dst, src_stream, accum):
    C, V, bw = tile_dims(t)
    ev_scr = E.now("eS")
    for j in range(4):
        bk = 0 if (j % 2 == 0) else 2
        w = bw[j]
        E.emit("tensor", None, waits=[("eV", T.get("ev_ptf%d" % (j % 2), 0)),
                                      ("eS", ev_scr)])
        for f in range(NMC):
            E.emit("tensor", lambda e, j=j, f=f, bk=bk, w=w: e.transpose(
                bank32(T["ps"], bk, 2)[0:w, P * f: P * (f + 1)],
                T["scr32"][:, CW * f + P * j: CW * f + P * j + w],
                T["idf"]), incs=[("eT", 1)])
        ev_t = E.now("eT")
        waits = [("eT", ev_t)]
        if dst is T["xB"] and T.get("ev_out_dma", 0):
            waits.append(("dOU", T["ev_out_dma"]))
        kw = {}
        ev = E.emit("vector", lambda e, j=j, bk=bk, w=w, kw=kw: e.scalar_tensor_tensor(
            dst[0:w, DM * j: DM * (j + 1)],
            bank32(T["ps"], bk, 2)[0:w, :],
            1.0,
            src_stream[0:w, DM * j: DM * (j + 1)],
            ALU.mult, ALU.add, **kw),
            waits=waits, incs=[("eV", 1)])
        T["ev_ptf%d" % (j % 2)] = ev["eV"]
        T["ev_stt_blk%d" % j] = ev["eV"]
    T["ev_stream"] = E.now("eV")


def emit_ffn(E, T, t, post_u1=None):
    C, V, bw = tile_dims(t)
    N = C
    ps = T["ps"]
    E.emit("tensor", None, waits=[("dFW", T["ev_ffnw"]), ("eV", T["ev_xnfm"]),
                                  ("eS", E.now("eS"))])
    for m in range(NRC):
        for k in range(NMC):
            E.emit("tensor", lambda e, m=m, k=k: e.matmul(
                bank32(ps, m)[:, 0:N],
                T["u1w"][:, R * k + P * m: R * k + P * (m + 1)],
                T["xnfm"][:, CW * k: CW * k + N],
                start=(k == 0), stop=(k == NMC - 1)), incs=[("eT", 1)])
    ev_u1 = E.now("eT")
    E.emit("scalar", lambda e: e.preload_act_table(AF.Gelu_apprx_tanh))
    for m in range(NRC):
        E.emit("scalar", lambda e, m=m: e.copy(
            T["pfm"][:, CW * m: CW * m + N], bank32(ps, m)[:, 0:N]),
            waits=[("eT", ev_u1)], incs=[("eS", 1)])
    ev_pfm = E.now("eS")
    if post_u1 is not None:
        post_u1()
    gelu_done, u2_done, v1_done = {}, {}, {}
    for f in range(NFC):
        pp = f % 4
        vb = pp
        tw = [("eS", ev_pfm)]
        if f >= 4:
            tw.append(("eS", gelu_done[f - 4]))
        E.emit("tensor", None, waits=tw)
        for k in range(NRC):
            E.emit("tensor", lambda e, f=f, k=k, vb=vb: e.matmul(
                bank32(ps, vb)[:, 0:N],
                T["v1w"][:, DFF * k + P * f: DFF * k + P * (f + 1)],
                T["pfm"][:, CW * k: CW * k + N],
                start=(k == 0), stop=(k == NRC - 1)), incs=[("eT", 1)])
        v1_done[f] = E.now("eT")
        if f >= 1:
            E.emit("tensor", None, waits=[("eS", gelu_done[f - 1])])
            for m in range(NRC):
                E.emit("tensor", lambda e, f=f, m=m: e.matmul(
                    bank32(ps, 6 + m)[:, 0:N],
                    T["u2w"][:, R * (f - 1) + P * m: R * (f - 1) + P * (m + 1)],
                    T["hh"][:, CW * ((f - 1) % 4): CW * ((f - 1) % 4) + N],
                    start=(f - 1 == 0), stop=False), incs=[("eT", 1)])
            u2_done[f - 1] = E.now("eT")
        sw = [("eT", v1_done[f])]
        if f >= 4:
            sw.append(("eT", u2_done.get(f - 4, 0)))
        ev = E.emit("scalar", lambda e, f=f, pp=pp, vb=vb: e.activation(
            T["hh"][:, CW * pp: CW * pp + N], bank32(ps, vb)[:, 0:N],
            AF.Gelu_apprx_tanh, bias=T["b1c"][:, f:f + 1]),
            waits=sw, incs=[("eS", 1)])
        gelu_done[f] = ev["eS"]
    f = NFC - 1
    E.emit("tensor", None, waits=[("eS", gelu_done[f])])
    for m in range(NRC):
        E.emit("tensor", lambda e, f=f, m=m: e.matmul(
            bank32(ps, 6 + m)[:, 0:N],
            T["u2w"][:, R * f + P * m: R * f + P * (m + 1)],
            T["hh"][:, CW * (f % 4): CW * (f % 4) + N],
            start=False, stop=True), incs=[("eT", 1)])
    u2_done[f] = E.now("eT")
    ev_u2 = E.now("eT")
    for m in range(NRC):
        E.emit("scalar", lambda e, m=m: e.copy(
            T["qfm"][:, CW * m: CW * m + N], bank32(ps, 6 + m)[:, 0:N]),
            waits=[("eT", ev_u2)], incs=[("eS", 1)])
    ev_qfm = E.now("eS")
    evac_done = {}
    for m in range(NMC):
        vb = 2 + (m % 2)
        tw = [("eS", ev_qfm)]
        if m >= 2:
            tw.append(("eS", evac_done[m - 2]))
        E.emit("tensor", None, waits=tw)
        for k in range(NRC):
            E.emit("tensor", lambda e, m=m, k=k, vb=vb: e.matmul(
                bank32(ps, vb)[:, 0:N],
                T["v2w"][:, DM * k + P * m: DM * k + P * (m + 1)],
                T["qfm"][:, CW * k: CW * k + N],
                start=(k == 0), stop=(k == NRC - 1)), incs=[("eT", 1)])
        ev_v2 = E.now("eT")
        ev = E.emit("scalar", lambda e, m=m, vb=vb: e.activation(
            T["scr32"][:, CW * m: CW * m + N], bank32(ps, vb)[:, 0:N],
            AF.Identity, bias=T["b2h"][:, m:m + 1], scale=T["halfc"]),
            waits=[("eT", ev_v2), ("eV", T["ev_b2h"])], incs=[("eS", 1)])
        evac_done[m] = ev["eS"]


def emit_attn(E, T, t):
    C, V, bw = tile_dims(t)
    ps = T["ps"]
    ipmm, cvmm = T["ipmm"], T["cvmm"]
    hevac, sgev, gstt, gmm = {}, {}, {}, {}
    ev_s_attn0 = E.now("eS")
    E.emit("scalar", lambda e: e.preload_act_table(AF.Silu))
    ngroups = G if ATTN_GROUPS is None else ATTN_GROUPS
    for g in range(ngroups):
        pp = g % 2
        key = (t, g)
        pkey = (t, g - 2) if g >= 2 else (t - 1, g + G - 2)
        sem = "dIP%d" % pp
        ev = E.emit("sync", lambda e, g=g, pp=pp: e.dma_start(
            out=T["wip"][:, 4096 * pp: 4096 * (pp + 1)],
            in_=T["s_ip"][:, 4096 * g: 4096 * (g + 1)]),
            waits=[T["ev_s_ip"], ("eT", ipmm.get(pkey, 0))],
            incs=[(sem, 16)])
        ip_rdy = (sem, ev[sem])
        sem = "dCV%d" % pp
        ev = E.emit("sync", lambda e, g=g, pp=pp: e.dma_start(
            out=T["wcv"][:, 2048 * pp: 2048 * (pp + 1)],
            in_=T["s_cv"][:, 2048 * g: 2048 * (g + 1)]),
            waits=[T["ev_s_cv"], ("eT", cvmm.get(pkey, 0))],
            incs=[(sem, 16)])
        cv_rdy = (sem, ev[sem])
        E.emit("tensor", None, waits=[ip_rdy, ("eV", T["ev_xnfm"]),
                                      ("eS", hevac.get(g - 1, ev_s_attn0))])
        for gi in range(2):
            for k in range(NMC):
                E.emit("tensor", lambda e, gi=gi, k=k, pp=pp: e.matmul(
                    bank32(ps, gi)[:, 0:C],
                    T["wip"][:, 4096 * pp + 512 * k + P * gi:
                             4096 * pp + 512 * k + P * (gi + 1)],
                    T["xnfm"][:, CW * k: CW * k + C],
                    start=(k == 0), stop=(k == NMC - 1)), incs=[("eT", 1)])
        hmm_done = E.now("eT")
        E.emit("scalar", None, waits=[("eT", hmm_done), ("eT", cvmm.get(pkey, 0))])
        for gi in range(2):
            E.emit("scalar", lambda e, gi=gi, pp=pp: e.copy(
                T["hgrp"][:, CW * (2 * pp + gi): CW * (2 * pp + gi) + C],
                bank32(ps, gi)[:, 0:C]), incs=[("eS", 1)])
        hevac[g] = E.now("eS")
        if t == 0:
            for gi in range(2):
                E.emit("vector", lambda e, gi=gi, pp=pp: e.tensor_mul(
                    T["hgrp"][:, CW * (2 * pp + gi): CW * (2 * pp + gi) + 3],
                    T["hgrp"][:, CW * (2 * pp + gi): CW * (2 * pp + gi) + 3],
                    T["maskb"][:, 0:3]),
                    waits=[("eS", hevac[g]), ("dMK", T["ev_mask"])], incs=[("eV", 1)])
            T["ev_hmask%d" % g] = E.now("eV")
        E.emit("tensor", None, waits=[("eS", sgev.get(g - 1, 0))])
        for gi in range(2):
            for k in range(NMC):
                E.emit("tensor", lambda e, gi=gi, k=k, pp=pp: e.matmul(
                    bank32(ps, 2 + gi)[:, 0:V],
                    T["wip"][:, 4096 * pp + 512 * k + 256 + P * gi:
                             4096 * pp + 512 * k + 256 + P * (gi + 1)],
                    T["xnfm"][:, CW * k + 3: CW * k + 3 + V],
                    start=(k == 0), stop=(k == NMC - 1)), incs=[("eT", 1)])
        gmm[g] = E.now("eT")
        ipmm[key] = gmm[g]
        E.emit("scalar", None, waits=[("eT", gmm[g]), ("eV", gstt.get(g - 1, 0))])
        for gi in range(2):
            E.emit("scalar", lambda e, gi=gi: e.activation(
                T["sgrp"][:, CW * gi: CW * gi + V], bank32(ps, 2 + gi)[:, 0:V],
                AF.Silu), incs=[("eS", 1)])
        sgev[g] = E.now("eS")
        cw = [cv_rdy, ("eS", hevac[g]), ("eV", gstt.get(g - 1, 0))]
        if t == 0:
            cw.append(("eV", T["ev_hmask%d" % g]))
        E.emit("tensor", None, waits=cw)
        for oc in range(2):
            idx = 0
            for k in range(KW):
                for ci in range(2):
                    E.emit("tensor", lambda e, oc=oc, k=k, ci=ci, pp=pp,
                           idx=idx: e.matmul(
                        bank32(ps, 4 + oc)[:, 0:V],
                        T["wcv"][:, 2048 * pp + 256 * (2 * k + ci) + P * oc:
                                 2048 * pp + 256 * (2 * k + ci) + P * (oc + 1)],
                        T["hgrp"][:, CW * (2 * pp + ci) + k:
                                  CW * (2 * pp + ci) + k + V],
                        start=(idx == 0), stop=(idx == 7)), incs=[("eT", 1)])
                    idx += 1
        cvmm[key] = E.now("eT")
        E.emit("vector", None, waits=[("eT", cvmm[key]), ("eS", sgev[g])])
        for oc in range(2):
            ch = 2 * g + oc
            E.emit("vector", lambda e, oc=oc, ch=ch: e.scalar_tensor_tensor(
                T["hgt"][:, CW * ch + 3: CW * ch + 3 + V],
                bank32(ps, 4 + oc)[:, 0:V],
                T["cbc"][:, ch:ch + 1], T["sgrp"][:, CW * oc: CW * oc + V],
                ALU.add, ALU.mult), incs=[("eV", 1)])
        gstt[g] = E.now("eV")
    if ATTN_GROUPS is not None:
        return
    sqd = {}
    sqmm = {}
    E.emit("tensor", None, waits=[("eV", gstt[0])])
    for ch in range(NIC):
        pp = ch % 2
        w = [("eV", gstt[ch // 2])]
        if ch >= 2:
            w.append(("eT", sqmm[ch - 2]))
        ev = E.emit("vector", lambda e, ch=ch, pp=pp: e.tensor_mul(
            T["sqpp"][:, CW * pp: CW * pp + V],
            T["hgt"][:, CW * ch + 3: CW * ch + 3 + V],
            T["hgt"][:, CW * ch + 3: CW * ch + 3 + V]),
            waits=w, incs=[("eV", 1)])
        sqd[ch] = ev["eV"]
        E.emit("tensor", lambda e, ch=ch: e.matmul(
            bank32(ps, 6)[0:1, 0:V], T["onesb"],
            T["hgt"][:, CW * ch + 3: CW * ch + 3 + V],
            start=(ch == 0), stop=(ch == NIC - 1)),
            waits=[("eV", gstt[ch // 2])], incs=[("eT", 1)])
        ev = E.emit("tensor", lambda e, ch=ch, pp=pp: e.matmul(
            bank32(ps, 7)[0:1, 0:V], T["onesb"],
            T["sqpp"][:, CW * pp: CW * pp + V],
            start=(ch == 0), stop=(ch == NIC - 1)),
            waits=[("eV", sqd[ch])], incs=[("eT", 1)])
        sqmm[ch] = ev["eT"]
    ev_stats = E.now("eT")
    E.emit("vector", lambda e: e.tensor_copy(
        T["row_sums"][0:1, 0:V], bank32(ps, 6)[0:1, 0:V]),
        waits=[("eT", ev_stats)], incs=[("eV", 1)])
    E.emit("vector", lambda e: e.tensor_copy(
        T["row_sums"][0:1, CW:CW + V], bank32(ps, 7)[0:1, 0:V]),
        incs=[("eV", 1)])
    ev_rows = vflush(E)
    jw = [min(P, V - P * jj) for jj in range(4)]
    E.emit("tensor", None, waits=[("eV", ev_rows)])
    for jj in range(4):
        E.emit("tensor", lambda e, jj=jj, w=jw[jj]: e.transpose(
            bank32(ps, 0)[0:w, jj:jj + 1],
            T["row_sums"][0:1, P * jj: P * jj + w], T["idf"][0:1, 0:1]),
            incs=[("eT", 1)])
        E.emit("tensor", lambda e, jj=jj, w=jw[jj]: e.transpose(
            bank32(ps, 0)[0:w, 4 + jj: 5 + jj],
            T["row_sums"][0:1, CW + P * jj: CW + P * jj + w], T["idf"][0:1, 0:1]),
            incs=[("eT", 1)])
    ev_tT = E.now("eT")
    E.emit("vector", lambda e: e.tensor_copy(T["lnT"], bank32(ps, 0)[:, 0:8]),
           waits=[("eT", ev_tT)], incs=[("eV", 1)])
    E.emit("vector", lambda e: e.tensor_scalar_mul(T["st_mu"], T["lnT"][:, 0:4], 1.0 / DI),
           waits=[("eV", E.now("eV"))], incs=[("eV", 1)])
    E.emit("vector", lambda e: e.tensor_mul(T["st_m2"], T["st_mu"], T["st_mu"]),
           waits=[("eV", E.now("eV"))], incs=[("eV", 1)])
    E.emit("vector", lambda e: e.scalar_tensor_tensor(
        T["st_var"], T["lnT"][:, 4:8], 1.0 / DI, T["st_m2"], ALU.mult, ALU.subtract),
        waits=[("eV", E.now("eV"))], incs=[("eV", 1)])
    E.emit("vector", lambda e: e.tensor_scalar(
        T["st_var"], T["st_var"], EPS, None, ALU.add),
        waits=[("eV", E.now("eV"))], incs=[("eV", 1)])
    E.emit("scalar", lambda e: e.preload_act_table(AF.Silu))
    emit_rsqrt(E, T, T["st_rs"], T["st_var"], T["st_sd"])
    E.emit("vector", lambda e: e.tensor_mul(T["st_am"], T["st_mu"], T["st_rs"]),
           waits=[("eV", E.now("eV"))], incs=[("eV", 1)])
    ev_ra = vflush(E)
    E.emit("tensor", None, waits=[("eV", ev_ra)])
    for jj in range(4):
        E.emit("tensor", lambda e, jj=jj, w=jw[jj]: e.transpose(
            bank32(ps, 1)[0:1, P * jj: P * jj + w],
            T["st_rs"][0:w, jj:jj + 1], T["idf"][0:w, 0:w]),
            incs=[("eT", 1)])
    ev_t1 = E.now("eT")
    E.emit("vector", lambda e: e.tensor_copy(
        T["row_ra"][0:1, 0:V], bank32(ps, 1)[0:1, 0:V]),
        waits=[("eT", ev_t1)], incs=[("eV", 1)])
    ev_e1 = vflush(E)
    E.emit("tensor", None, waits=[("eV", ev_e1)])
    for jj in range(4):
        E.emit("tensor", lambda e, jj=jj, w=jw[jj]: e.transpose(
            bank32(ps, 1)[0:1, P * jj: P * jj + w],
            T["st_am"][0:w, jj:jj + 1], T["idf"][0:w, 0:w]),
            incs=[("eT", 1)])
    ev_t2 = E.now("eT")
    E.emit("vector", lambda e: e.tensor_copy(
        T["row_ra"][0:1, CW:CW + V], bank32(ps, 1)[0:1, 0:V]),
        waits=[("eT", ev_t2)], incs=[("eV", 1)])
    ev_e2 = vflush(E)
    E.emit("tensor", None, waits=[("eV", ev_e2)])
    E.emit("tensor", lambda e: e.matmul(
        bank32(ps, 2)[:, 0:V], T["onesr"], T["row_ra"][0:1, 0:V],
        start=True, stop=True), incs=[("eT", 1)])
    E.emit("tensor", lambda e: e.matmul(
        bank32(ps, 3)[:, 0:V], T["onesr"], T["row_ra"][0:1, CW:CW + V],
        start=True, stop=True), incs=[("eT", 1)])
    ev_bc = E.now("eT")
    apl = {}
    for ch in range(NIC):
        tmp = T["scr32"][:, (6 + ch % 2) * CW: (6 + ch % 2) * CW + V]
        w = [("eT", ev_bc)]
        if ch >= 2:
            w.append(("eS", apl[ch - 2]))
        E.emit("vector", lambda e, ch=ch, tmp=tmp: e.tensor_mul(
            tmp, T["hgt"][:, CW * ch + 3: CW * ch + 3 + V], bank32(ps, 2)[:, 0:V]),
            waits=w, incs=[("eV", 1)])
        ev = E.emit("vector", lambda e, tmp=tmp: e.tensor_sub(
            tmp, tmp, bank32(ps, 3)[:, 0:V]), incs=[("eV", 1)])
        ev2 = E.emit("scalar", lambda e, ch=ch, tmp=tmp: e.activation(
            T["hnb"][:, CW * ch + 3: CW * ch + 3 + V], tmp,
            AF.Identity, bias=T["lbc"][:, ch:ch + 1], scale=T["lgc"][:, ch:ch + 1]),
            waits=[("eV", ev["eV"])], incs=[("eS", 1)])
        apl[ch] = ev2["eS"]
    ev_hn = E.now("eS")
    if ATTN_PHASE == 1:
        return
    fcmm = T["fcmm"]
    fcev = {}
    for m in range(NMC):
        pp = m % 2
        pkey = (t, m - 2) if m >= 2 else (t - 1, m + NMC - 2)
        sem = "dFC%d" % pp
        ev = E.emit("sync", lambda e, m=m, pp=pp: e.dma_start(
            out=T["wfc"][:, 2048 * pp: 2048 * (pp + 1)],
            in_=T["s_fc"][:, 2048 * m: 2048 * (m + 1)]),
            waits=[T["ev_s_fc"], ("eT", fcmm.get(pkey, 0))],
            incs=[(sem, 16)])
        tw = [(sem, ev[sem]), ("eS", ev_hn)]
        if m >= 2:
            tw.append(("eS", fcev[m - 2]))
        E.emit("tensor", None, waits=tw)
        for k in range(NIC):
            E.emit("tensor", lambda e, m=m, k=k, pp=pp: e.matmul(
                bank32(ps, 4 + (m % 2))[:, 0:C],
                T["wfc"][:, 2048 * pp + P * k: 2048 * pp + P * (k + 1)],
                T["hnb"][:, CW * k: CW * k + C],
                start=(k == 0), stop=(k == NIC - 1)), incs=[("eT", 1)])
        fcmm[(t, m)] = E.now("eT")
        ev = E.emit("scalar", lambda e, m=m: e.activation(
            T["sfc"][:, CW * m: CW * m + C], bank32(ps, 4 + (m % 2))[:, 0:C],
            AF.Silu, bias=T["fbc"][:, m:m + 1]),
            waits=[("eT", fcmm[(t, m)])], incs=[("eS", 1)])
        fcev[m] = ev["eS"]
    if ATTN_PHASE == 2:
        return
    cpmm = T["cpmm"]
    cpev = {}
    for m in range(NIC):
        pp = m % 2
        pkey = (t, m - 2) if m >= 2 else (t - 1, m + NIC - 2)
        sem = "dCP%d" % pp
        ev = E.emit("sync", lambda e, m=m, pp=pp: e.dma_start(
            out=T["wcp"][:, 1024 * pp: 1024 * (pp + 1)],
            in_=T["s_cp"][:, 1024 * m: 1024 * (m + 1)]),
            waits=[T["ev_s_cp"], ("eT", cpmm.get(pkey, 0))],
            incs=[(sem, 16)])
        tw = [(sem, ev[sem]), ("eS", fcev[NMC - 1])]
        if m >= 2:
            tw.append(("eV", cpev[m - 2]))
        E.emit("tensor", None, waits=tw)
        for k in range(NMC):
            E.emit("tensor", lambda e, m=m, k=k, pp=pp: e.matmul(
                bank32(ps, 6 + (m % 2))[:, 0:C],
                T["wcp"][:, 1024 * pp + P * k: 1024 * pp + P * (k + 1)],
                T["sfc"][:, CW * k: CW * k + C],
                start=(k == 0), stop=(k == NMC - 1)), incs=[("eT", 1)])
        cpmm[(t, m)] = E.now("eT")
        ev = E.emit("vector", lambda e, m=m: e.scalar_tensor_tensor(
            T["hnb"][:, CW * m: CW * m + C], bank32(ps, 6 + (m % 2))[:, 0:C],
            T["cpc"][:, m:m + 1], T["hgt"][:, CW * m: CW * m + C],
            ALU.add, ALU.add),
            waits=[("eT", cpmm[(t, m)])], incs=[("eV", 1)])
        cpev[m] = ev["eV"]
    ev_h2 = E.now("eV")
    if ATTN_PHASE == 3:
        return
    owmm = T["owmm"]
    owev = {}
    for m in range(NMC):
        pp = m % 2
        pkey = (t, m - 2) if m >= 2 else (t - 1, m + NMC - 2)
        sem = "dOW%d" % pp
        ev = E.emit("sync", lambda e, m=m, pp=pp: e.dma_start(
            out=T["wow"][:, 2048 * pp: 2048 * (pp + 1)],
            in_=T["s_ow"][:, 2048 * m: 2048 * (m + 1)]),
            waits=[T["ev_s_ow"], ("eT", owmm.get(pkey, 0))],
            incs=[(sem, 16)])
        tw = [(sem, ev[sem]), ("eV", ev_h2)]
        if m >= 2:
            tw.append(("eS", owev[m - 2]))
        E.emit("tensor", None, waits=tw)
        for k in range(NIC):
            E.emit("tensor", lambda e, m=m, k=k, pp=pp: e.matmul(
                bank32(ps, (m % 2))[:, 0:C],
                T["wow"][:, 2048 * pp + P * k: 2048 * pp + P * (k + 1)],
                T["hnb"][:, CW * k: CW * k + C],
                start=(k == 0), stop=(k == NIC - 1)), incs=[("eT", 1)])
        owmm[(t, m)] = E.now("eT")
        ev = E.emit("scalar", lambda e, m=m: e.copy(
            T["scr32"][:, CW * m: CW * m + C], bank32(ps, (m % 2))[:, 0:C]),
            waits=[("eT", owmm[(t, m)])], incs=[("eS", 1)])
        owev[m] = ev["eS"]


def emit_xout(E, T, t):
    C, V, bw = tile_dims(t)
    s = STARTS[t]
    acc = 0
    ev = None
    for j in range(4):
        rlo = 3 if j == 0 else 0
        w = (bw[j] - 3) if j == 0 else bw[j]
        if w <= 0:
            continue
        ev = E.emit("gpsimd", lambda e, j=j, rlo=rlo, w=w, o=acc, s=s: e.dma_start(
            out=T["d_out"][s + o: s + o + w, :],
            in_=T["xB"][rlo: rlo + w, DM * j: DM * (j + 1)]),
            waits=[("eV", T["ev_stream"])] if j == 0 else (),
            incs=[("dOU", 16)])
        acc += w
    assert acc == V
    T["ev_out_dma"] = ev["dOU"]


def emit_tile(E, T, t):
    if not T.pop("t1_done", False):
        emit_xpose_xn(E, T, t)
    emit_ffn(E, T, t)
    emit_stream_add(E, T, t, T["xB"], T["xC"], accum=True)
    T["ev_xC_free_v"] = E.now("eV")
    emit_norm23(E, T, t, "xB", 2)
    emit_xpose_xn(E, T, t)
    emit_attn(E, T, t)
    if t + 1 < NT:
        emit_xin(E, T, t + 1)
        emit_norm1_stats(E, T, t + 1)
    emit_stream_add(E, T, t, T["xA"], T["xB"], accum=True)
    emit_norm23(E, T, t, "xA", 3)
    emit_xpose_xn(E, T, t)

    def post_u1():
        if t + 1 < NT:
            emit_norm1_apply(E, T, t + 1)
            emit_xpose_xn(E, T, t + 1)
            T["t1_done"] = True

    emit_ffn(E, T, t, post_u1=post_u1)
    emit_stream_add(E, T, t, T["xB"], T["xA"], accum=False)
    emit_xout(E, T, t)



def _chunk_cols(v, n):
    return np.ascontiguousarray(v.reshape(n, P).T).astype(np.float32)


def stage_inputs(inp):
    x = np.asarray(inp["L0_attn_hidden_states"], dtype=np.float32)
    ipw = np.asarray(inp["L0_attn_in_proj_w"], dtype=np.float32)
    cvw = np.asarray(inp["L0_attn_conv_w"], dtype=np.float32)
    fcw = np.asarray(inp["L0_attn_fc_w"], dtype=np.float32)
    cpw = np.asarray(inp["L0_attn_cproj_w"], dtype=np.float32)
    oww = np.asarray(inp["L0_attn_out_w"], dtype=np.float32)

    t1 = ipw.T.reshape(NMC, P, 32, P)
    gidx = np.array([[2 * g, 2 * g + 1, 16 + 2 * g, 17 + 2 * g] for g in range(G)])
    iph = t1[:, :, gidx.reshape(-1), :].reshape(NMC, P, G, 4, P)
    iph = iph.transpose(1, 2, 0, 3, 4).reshape(P, G * 4096)

    a = cvw.transpose(2, 1, 0).reshape(KW, 2, P, 16, P)
    cvh = a.transpose(2, 0, 1, 3, 4)
    cvh = np.stack([cvh[:, :, :, 2 * g: 2 * g + 2, :].reshape(P, 2048)
                    for g in range(G)], axis=1).reshape(P, G * 2048)

    fch = fcw.T.reshape(NIC, P, NMC, P).transpose(1, 2, 0, 3).reshape(P, NMC * 2048)
    cph = cpw.T.reshape(NMC, P, NIC, P).transpose(1, 2, 0, 3).reshape(P, NIC * 1024)
    owh = oww.T.reshape(NIC, P, NMC, P).transpose(1, 2, 0, 3).reshape(P, NMC * 2048)

    u1h = np.asarray(inp["L0_ffn_U1"], np.float32).reshape(NMC, P, R) \
        .transpose(1, 0, 2).reshape(P, NMC * R)
    v1h = np.asarray(inp["L0_ffn_V1"], np.float32).reshape(NRC, P, DFF) \
        .transpose(1, 0, 2).reshape(P, NRC * DFF)
    u2h = np.asarray(inp["L0_ffn_U2"], np.float32).reshape(NFC, P, R) \
        .transpose(1, 0, 2).reshape(P, NFC * R)
    v2h = np.asarray(inp["L0_ffn_V2"], np.float32).reshape(NRC, P, DM) \
        .transpose(1, 0, 2).reshape(P, NRC * DM)

    common = {
        "ip_in": np.ascontiguousarray(iph), "cv_in": np.ascontiguousarray(cvh),
        "fc_in": np.ascontiguousarray(fch), "cp_in": np.ascontiguousarray(cph),
        "ow_in": np.ascontiguousarray(owh),
        "u1_in": np.ascontiguousarray(u1h), "v1_in": np.ascontiguousarray(v1h),
        "u2_in": np.ascontiguousarray(u2h), "v2_in": np.ascontiguousarray(v2h),
        "b1_in": _chunk_cols(np.asarray(inp["L0_ffn_b1"], np.float32), NFC),
        "b2_in": _chunk_cols(np.asarray(inp["L0_ffn_b2"], np.float32), NMC),
        "cb_in": _chunk_cols(np.asarray(inp["L0_attn_conv_b"], np.float32), NIC),
        "fb_in": _chunk_cols(np.asarray(inp["L0_attn_fc_b"], np.float32), NMC),
        "cpb_in": _chunk_cols(np.asarray(inp["L0_attn_cproj_b"], np.float32), NIC),
        "lg_in": _chunk_cols(np.asarray(inp["L0_attn_ln_g"], np.float32), NIC),
        "lb_in": _chunk_cols(np.asarray(inp["L0_attn_ln_b"], np.float32), NIC),
    }
    in_maps = []
    for c in range(NCORES):
        b, h = c // 2, c % 2
        lo = 2048 * h
        if h == 0:
            halo = np.zeros((3, DM), dtype=np.float32)
            mask = np.zeros((P, 4), dtype=np.float32)
        else:
            halo = x[b, lo - 3: lo]
            mask = np.ones((P, 4), dtype=np.float32)
        xi = np.ascontiguousarray(np.concatenate([halo, x[b, lo: lo + 2048]], axis=0))
        m = dict(common)
        m["x_in"] = xi
        m["mask_in"] = mask
        in_maps.append(m)
    return in_maps


def kernel(**inputs):
    in_maps = stage_inputs(inputs)
    nc = build_nc()
    res = run_bass_kernel_spmd(nc, in_maps, core_ids=list(range(NCORES)))
    out = np.empty((4, 4096, DM), dtype=np.float32)
    for c in range(NCORES):
        b, h = c // 2, c % 2
        out[b, 2048 * h: 2048 * (h + 1)] = np.asarray(res.results[c]["x_out"])
    return out
